# version 19
# baseline (speedup 1.0000x reference)
import numpy as np
from contextlib import ExitStack

import concourse.bass as bass
import concourse.mybir as mybir
from concourse.bass_utils import run_bass_kernel_spmd

F32 = mybir.dt.float32
BF16 = mybir.dt.bfloat16
ALU = mybir.AluOpType
AF = mybir.ActivationFunctionType
AX = mybir.AxisListType


class Cfg:
    def __init__(self, H=2048, T=16384, INTER=8192, NC=8, C=256, NW=2,
                 EPS=1e-5):
        self.H, self.T, self.INTER, self.NC, self.C, self.NW, self.EPS = (
            H, T, INTER, NC, C, NW, EPS)
        self.G = H // 128
        self.GI = INTER // 128
        self.TLOC = T // NC
        self.NMAIN = self.TLOC // C
        self.W = NW * C
        self.NBLK = max(1, INTER // 2048)
        self.ITB = self.GI // self.NBLK
        assert H % 128 == 0 and INTER % 128 == 0 and self.TLOC % C == 0
        assert C % 128 == 0 and self.GI % self.NBLK == 0


ENGS = ("sync", "tensor", "vector", "scalar", "gpsimd")


class Ctr:
    def __init__(self, sem):
        self.sem = sem
        self.n = 0


class Builder:
    def __init__(self, nc, es, cfg):
        self.nc, self.es, self.cfg = nc, es, cfg
        self.prog = {e: [] for e in ENGS}
        self.ctr = {e: Ctr(es.enter_context(nc.semaphore(f"c_{e}")))
                    for e in ENGS}
        self.cout = Ctr(es.enter_context(nc.semaphore("c_out")))
        self.dirty = {e: set() for e in ENGS}
        self.touched = {e: set() for e in ENGS}
        self.waited = {}
        self.pguard = {}
        self.streams = {}

    def stream(self, name):
        if name not in self.streams:
            self.streams[name] = Ctr(
                self.es.enter_context(self.nc.semaphore("st_" + name)))
        return self.streams[name]

    def emit(self, eng, f):
        self.prog[eng].append(f)

    def wait(self, eng, ctr, val=None):
        v = ctr.n if val is None else val
        if v <= 0:
            return
        key = (eng, ctr.sem.name)
        if self.waited.get(key, -1) >= v:
            return
        self.waited[key] = v
        self.emit(eng, lambda e, s=ctr.sem, v=v: e.wait_ge(s, v))

    def wait_all(self, eng):
        for name in ENGS:
            if name != eng:
                self.wait(eng, self.ctr[name])

    def barrier(self):
        for e in ENGS:
            self.wait_all(e)
            self.wait(e, self.cout)
            for s in self.streams.values():
                self.wait(e, s)

    @staticmethod
    def _names(aps):
        out = set()
        for a in aps:
            if a is None or isinstance(a, (int, float)):
                continue
            out.add(a.tensor.name)
        return out

    def _predrain(self, eng, outs, ins):
        innames = self._names(ins)
        outnames = self._names(outs)
        if (innames & self.dirty[eng]) or (outnames & self.touched[eng]):
            self.emit(eng, lambda e: e.drain())
            self.dirty[eng] = set()
            self.touched[eng] = set()
        self.dirty[eng] |= outnames
        self.touched[eng] |= outnames | innames

    def op(self, eng, outs, ins, f, inc=True):
        self._predrain(eng, outs, ins)
        c = self.ctr[eng]
        if inc:
            c.n += 1
            self.emit(eng, lambda e, f=f, s=c.sem: f(e).then_inc(s, 1))
        else:
            self.emit(eng, lambda e, f=f: f(e))

    def dma(self, eng, out, in_, ctr=None):
        if eng in ("vector", "scalar", "gpsimd", "tensor"):
            innames = self._names([in_])
            if innames & self.dirty[eng]:
                self.emit(eng, lambda e: e.drain())
                self.dirty[eng] = set()
                self.touched[eng] = set()
        c = ctr if ctr is not None else self.ctr[eng]
        c.n += 16
        self.emit(eng, lambda e, o=out, i=in_, s=c.sem:
                  e.dma_start(out=o, in_=i).then_inc(s, 16))

    PSUM_CONFLICTS = {
        "TR": ("TR", "A", "B", "Cs", "Ds"),
        "A": ("A", "TR"), "B": ("B", "TR"),
        "Cs": ("Cs", "TR"), "Ds": ("Ds", "TR"),
        "PR0": ("PR0", "b0", "b1"), "PR1": ("PR1", "b2", "b3"),
        "b0": ("b0", "PR0"), "b1": ("b1", "PR0"),
        "b2": ("b2", "PR1"), "b3": ("b3", "PR1"),
    }

    def psum_acquire(self, key, eng="tensor"):
        for k in self.PSUM_CONFLICTS.get(key, (key,)):
            g = self.pguard.get(k)
            if g is not None:
                ctr, val = g
                self.wait(eng, ctr, val)

    def psum_release(self, key, eng):
        self.pguard[key] = (self.ctr[eng], self.ctr[eng].n)

    def v_tt(self, out, a, b, alu, inc=True):
        self.op("vector", [out], [a, b],
                lambda e: e.tensor_tensor(out=out, in0=a, in1=b, op=alu), inc)

    def v_ts1(self, out, a, s1, op0, inc=True):
        sc = None if isinstance(s1, float) else s1
        self.op("vector", [out], [a, sc],
                lambda e: e.tensor_single_scalar(out, a, s1, op0), inc)

    def v_ts2(self, out, a, s1, s2, op0, op1, inc=True):
        i2 = [a] + [x for x in (s1, s2) if not isinstance(x, float)]
        self.op("vector", [out], i2,
                lambda e: e.tensor_scalar(out, a, s1, s2, op0, op1), inc)

    def v_stt(self, out, in0, scal, in1, op0, op1, inc=True):
        self.op("vector", [out], [in0, scal, in1],
                lambda e: e.scalar_tensor_tensor(out, in0, scal, in1, op0, op1),
                inc)

    def v_copy(self, out, a, inc=True):
        self.op("vector", [out], [a], lambda e: e.tensor_copy(out, a), inc)

    def v_scan(self, out, d0, d1, init, op0, op1, inc=True):
        self.op("vector", [out], [d0, d1, init],
                lambda e: e.tensor_tensor_scan(out=out, data0=d0, data1=d1,
                                               initial=init, op0=op0, op1=op1),
                inc)

    def v_rmax(self, out, a, inc=True):
        self.op("vector", [out], [a],
                lambda e: e.reduce_max(out, a, axis=AX.X), inc)

    def v_memset(self, out, val, inc=True):
        self.op("vector", [out], [], lambda e: e.memset(out, val), inc)

    def v_recip(self, out, a, inc=True):
        self.op("vector", [out], [a], lambda e: e.reciprocal(out, a), inc)

    def a_act(self, out, a, fn, bias=0.0, scale=1.0, inc=True):
        ins = [a] + [x for x in (bias, scale) if not isinstance(x, float)]
        self.op("scalar", [out], ins,
                lambda e: e.activation(out=out, in_=a, func=fn, bias=bias,
                                       scale=scale), inc)

    def a_copy(self, out, a, inc=True):
        self.op("scalar", [out], [a], lambda e: e.copy(out=out, in_=a), inc)

    def g_tt(self, out, a, b, alu, inc=True):
        self.op("gpsimd", [out], [a, b],
                lambda e: e.tensor_tensor(out=out, in0=a, in1=b, op=alu), inc)

    def e_tt(self, eng, out, a, b, alu, inc=True):
        self.op(eng, [out], [a, b],
                lambda e: e.tensor_tensor(out=out, in0=a, in1=b, op=alu), inc)

    def e_ts1(self, eng, out, a, s1, op0, inc=True):
        sc = None if isinstance(s1, float) else s1
        self.op(eng, [out], [a, sc],
                lambda e: e.tensor_single_scalar(out, a, s1, op0), inc)

    def e_ts2(self, eng, out, a, s1, s2, op0, op1, inc=True):
        i2 = [a] + [x for x in (s1, s2) if not isinstance(x, float)]
        self.op(eng, [out], i2,
                lambda e: e.tensor_scalar(out, a, s1, s2, op0, op1), inc)

    def e_stt(self, eng, out, in0, scal, in1, op0, op1, inc=True):
        self.op(eng, [out], [in0, scal, in1],
                lambda e: e.scalar_tensor_tensor(out, in0, scal, in1, op0,
                                                 op1), inc)

    def e_copy(self, eng, out, a, inc=True):
        self.op(eng, [out], [a], lambda e: e.tensor_copy(out, a), inc)

    def mm(self, out, lhsT, rhs, start, stop, inc=False):
        c = self.ctr["tensor"]
        if inc:
            c.n += 1
            self.emit("tensor", lambda e, o=out, l=lhsT, r=rhs, st=start,
                      sp=stop, s=c.sem:
                      e.matmul(o, l, r, start=st, stop=sp).then_inc(s, 1))
        else:
            self.emit("tensor",
                      lambda e, o=out, l=lhsT, r=rhs, st=start, sp=stop:
                      e.matmul(o, l, r, start=st, stop=sp))


class SlabStream:

    NB = 4

    def __init__(self, B, slab_ap_bf16, unit_cols):
        self.B = B
        self.ap = slab_ap_bf16
        self.uc = unit_cols
        self.pe_done = [0] * self.NB
        self.kt = [0] * self.NB
        self.st = [B.stream(f"slab{i}") for i in range(self.NB)]
        self.rr = 0

    def next(self):
        b = self.rr % self.NB
        self.rr += 1
        return b

    def load(self, buf, kt, src_hi, src_lo):
        B = self.B
        self.kt[buf] = kt
        B.wait("sync", B.ctr["tensor"], self.pe_done[buf])
        base = buf * self.uc
        B.dma("sync", self.ap[:, base:base + kt * 128], src_hi,
              ctr=self.st[buf])
        B.dma("sync", self.ap[:, base + kt * 128:base + 2 * kt * 128],
              src_lo, ctr=self.st[buf])

    def tile(self, buf, k, hilo):
        o = buf * self.uc + hilo * self.kt[buf] * 128 + k * 128
        return self.ap[:, o:o + 128]

    def consumed(self, buf):
        self.pe_done[buf] = self.B.ctr["tensor"].n


def build_nc(cfg: Cfg):
    H, G, C = cfg.H, cfg.G, cfg.C
    GI, NBLK, ITB = cfg.GI, cfg.NBLK, cfg.ITB
    NW, NMAIN, TLOC = cfg.NW, cfg.NMAIN, cfg.TLOC
    NTOK = NW * C + 1 + TLOC
    C1 = C + 1
    KTH = G
    SLABMAX = max(KTH, ITB) * 256

    nc = bass.Bass("TRN2", target_bir_lowering=False, debug=False)

    d_x = nc.dram_tensor("x", [H, NTOK], F32, kind="ExternalInput").ap()
    d_w = {}
    for nm, shp in (("Wk", [H, H]), ("Wv", [H, H]), ("Wr", [H, H]),
                    ("Wo", [H, H]), ("Wgu", [2 * cfg.INTER, H]),
                    ("Wd", [H, cfg.INTER]), ("wmod", [H, 6 * H])):
        d_w[nm] = nc.dram_tensor(nm, shp, F32, kind="ExternalInput").ap()
    d_cols = {}
    for nm in ("c_col", "lnw_col", "lnb_col", "mixk_col", "mixv_col",
               "mixr_col", "td_col", "u_col"):
        d_cols[nm] = nc.dram_tensor(nm, [128, G], F32, kind="ExternalInput").ap()
    d_cmask = nc.dram_tensor("cmask", [128, 1], F32, kind="ExternalInput").ap()
    d_eye = nc.dram_tensor("eye", [128, 128], F32, kind="ExternalInput").ap()
    d_out = nc.dram_tensor("out", [H, TLOC], F32, kind="ExternalOutput").ap()

    def tape(name, O, I):
        return tuple(
            nc.dram_tensor(f"{name}{hl}", [O // 128, 128, (I // 128) * 128],
                           BF16, kind="Internal").ap() for hl in (0, 1))

    t_w = {
        "Wk": tape("t_wk", H, H), "Wv": tape("t_wv", H, H),
        "Wr": tape("t_wr", H, H), "Wo": tape("t_wo", H, H),
        "Wgu": tape("t_wgu", 2 * cfg.INTER, H),
    }
    t_wd = tuple(
        nc.dram_tensor(f"t_wd{hl}", [NBLK, G, 128, ITB * 128], BF16,
                       kind="Internal").ap() for hl in (0, 1))
    d_modsc = nc.dram_tensor("modsc", [6 * G, 128], F32, kind="Internal").ap()

    es = ExitStack()
    with es:
        ent = es.enter_context

        def sb(name, cols, dt=F32):
            return ent(nc.sbuf_tensor(name, [128, cols], dt))

        GC = G * C
        GC1 = G * C1
        P1 = sb("P1", GC1)
        P2 = sb("P2", GC1)
        P3 = sb("P3", max(GC1, ITB * C, 10 * C))
        P4 = sb("P4", max(GC1, 1280))
        QP0 = min(1024, (max(GC, 2 * G * 128) // 2 // 128) * 128)
        PA = sb("PA", 2 * GC, BF16)
        APR = sb("APR", max(2 * ITB * C, 4 * QP0), BF16)
        P6 = sb("P6", max(GC, 2 * G * 128))
        P7 = sb("P7", max(GC, G * 128, ITB * C))
        P9 = sb("P9", max(GC, 5 * C))
        P10 = sb("P10", GC1)
        P11 = sb("P11", GC1)
        P13 = sb("P13", 8 * min(8, max(KTH, ITB)) * 128, BF16)

        cols = {}
        for nm in ("c_col", "lnw_col", "lnb_col", "mixk_col", "mixv_col",
                   "mixr_col", "td_col", "u_col"):
            cols[nm] = sb("s_" + nm, G)
        cmask = sb("cmaskb", 1)
        eye = sb("eyeb", 128)
        lam = sb("lam", G)
        expu = sb("expu", G)
        kappa = sb("kappa", G)
        negkap = sb("negkap", G)
        ncar = sb("ncar", G)
        dcar = sb("dcar", G)
        xbcar = sb("xbcar", G)
        modc = sb("modc", 6 * G)
        opca = sb("opca", G)
        opcb = sb("opcb", G)
        gacl = sb("gacl", G)
        gbcl = sb("gbcl", G)
        scol = sb("scol", 4 * G)
        ones_row = ent(nc.sbuf_tensor("ones_row", [1, 128], F32))
        ones_col = ent(nc.sbuf_tensor("ones_col", [128, 1], F32))
        rows = ent(nc.sbuf_tensor("rows", [1, 6 * C1], F32))

        def row(i, cw):
            return rows.ap()[0:1, i * C1:i * C1 + cw]


        ps_rows = ent(nc.psum_tensor("ps_rows", [128, 1024], F32))
        ps_bc = ent(nc.psum_tensor("ps_bc", [128, 1024], F32))
        ps_big = ent(nc.psum_tensor("ps_big", [128, 2048], F32))

        B = Builder(nc, es, cfg)

        x_fm = P1.ap()
        h_fm = P2.ap()
        xtld = P3.ap()
        sq = P4.ap()
        kbuf = P6.ap()
        vbuf = P7.ap()
        sigr = P3.ap()
        ekbuf = P9.ap()
        N3 = P10.ap()
        D3 = P11.ap()

        MMSLOT = {
            "A": ps_big.ap()[:, 0:512],
            "B": ps_big.ap()[:, 512:1024],
            "Cs": ps_big.ap()[:, 1024:1536],
            "Ds": ps_big.ap()[:, 1536:2048],
        }
        KSUB = min(8, max(KTH, ITB))
        slabs = SlabStream(B, P13.ap(), KSUB * 256)

        def grp(ap_, g, cw, width=None):
            w = cw if width is None else width
            return ap_[:, g * cw:g * cw + w]

        def col3(ap_, cw):
            return ap_[:, 0:G * cw].rearrange("p (g c) -> p g c",
                                              c=cw)[:, :, 0:1]

        st_cols = B.stream("cols")
        for nm in cols:
            B.dma("sync", cols[nm].ap(), d_cols[nm], ctr=st_cols)
        B.dma("sync", cmask.ap(), d_cmask, ctr=st_cols)
        B.dma("sync", eye.ap(), d_eye, ctr=st_cols)
        B.wait("vector", st_cols)
        B.wait("scalar", st_cols)
        B.a_act(scol.ap()[:, 0:G], cols["td_col"].ap(), AF.Exp)
        B.a_act(lam.ap(), scol.ap()[:, 0:G], AF.Exp, scale=-1.0)
        B.a_act(expu.ap(), cols["u_col"].ap(), AF.Exp)
        B.v_memset(kappa.ap(), -1e30)
        B.v_memset(ncar.ap(), 0.0)
        B.v_memset(dcar.ap(), 0.0)
        B.v_memset(xbcar.ap(), 0.0)
        B.v_memset(ones_row.ap(), 1.0)
        B.v_memset(ones_col.ap(), 1.0)

        B.a_act(scol.ap()[:, G:2 * G], cols["c_col"].ap(), AF.Sigmoid)
        B.wait("vector", B.ctr["scalar"])
        B.v_tt(scol.ap()[:, 2 * G:3 * G], cols["c_col"].ap(),
               scol.ap()[:, G:2 * G], ALU.mult)
        siluc = scol.ap()[:, 2 * G:3 * G]
        st_wm = [B.stream("wm0"), B.stream("wm1")]
        st_wmst = B.stream("wmst")
        wm_pe_done = [0, 0]
        mrow = rows.ap()[0:1, 0:512]
        NSL = 6 * H // 512
        mod_ps = [ps_bc.ap()[0:1, 0:512], ps_bc.ap()[0:1, 512:1024]]
        mod_key = ["b2", "b3"]

        def mod_step(ns):
            def go():
                B.psum_acquire(mod_key[ns % 2])
                ps = mod_ps[ns % 2]
                for kk in range(KTH):
                    half = kk % 2
                    wmtile = P4.ap()[:, half * 512:half * 512 + 512]
                    B.wait("sync", B.ctr["tensor"], wm_pe_done[half])
                    B.dma("sync", wmtile,
                          d_w["wmod"][kk * 128:(kk + 1) * 128,
                                      ns * 512:(ns + 1) * 512],
                          ctr=st_wm[half])
                    B.wait("tensor", st_wm[half])
                    B.mm(ps, siluc[:, kk:kk + 1], wmtile,
                         start=(kk == 0), stop=(kk == KTH - 1), inc=True)
                    wm_pe_done[half] = B.ctr["tensor"].n
                B.wait("scalar", st_wmst)
                B.wait("scalar", B.ctr["tensor"])
                B.a_copy(mrow, ps)
                B.psum_release(mod_key[ns % 2], "scalar")
                B.dma("scalar", d_modsc[4 * ns:4 * ns + 4, :].flatten(),
                      mrow, ctr=st_wmst)
            return go

        mod_steps = [mod_step(ns) for ns in range(NSL)]

        B.barrier()
        QP = QP0
        NPN = 4 if P2.shape[1] >= 2 * QP else 2
        st_pnat = [B.stream(f"pnat{j}") for j in range(4)]
        st_store = [B.stream("pstore0"), B.stream("pstore1")]
        prep_state = {"pe": [0, 0, 0, 0], "ve": [0, 0], "i": 0}
        PTR_PS = [ps_big.ap()[:, 0:1024], ps_big.ap()[:, 1024:2048],
                  ps_rows.ap()[:, 0:1024]]

        def prep_slab(src, nk, dsth, dstl, late=False):
            i = prep_state["i"]
            prep_state["i"] += 1
            if mod_steps and i % 12 == 8:
                mod_steps.pop(0)()
            hb = i % 2
            if late:
                hp = "PR" + str(i % 2)
                pstr_t = (ps_rows if i % 2 == 0 else ps_bc).ap()
                base = hb * 2 * QP
                pnat = P3.ap()[:, base:base + nk * 128]
                phi = APR.ap()[:, hb * QP:hb * QP + nk * 128]
                plo = APR.ap()[:, 2 * QP + hb * QP:2 * QP + hb * QP + nk * 128]
            else:
                hp = "PTR" + str(i % 3)
                pstr_t = PTR_PS[i % 3]
                hb4 = i % NPN
                pn_t = (P6 if hb4 < 2 else P2).ap()
                pnat = pn_t[:, (hb4 % 2) * QP:(hb4 % 2) * QP + nk * 128]
                phi = APR.ap()[:, hb * QP:hb * QP + nk * 128]
                plo = APR.ap()[:, 2 * QP + hb * QP:2 * QP + hb * QP + nk * 128]
            pstr = pstr_t[:, 0:nk * 128]
            pb = hb if late else (i % NPN)
            B.wait("sync", B.ctr["tensor"], prep_state["pe"][pb])
            B.dma("sync", pnat, src, ctr=st_pnat[pb])
            B.wait("tensor", st_pnat[pb])
            B.psum_acquire(hp)
            for k in range(nk):
                B.mm(pstr[:, k * 128:(k + 1) * 128],
                     pnat[:, k * 128:(k + 1) * 128], eye.ap(),
                     start=True, stop=True, inc=(k == nk - 1))
            prep_state["pe"][pb] = B.ctr["tensor"].n
            B.wait("scalar", B.ctr["tensor"])
            B.wait("scalar", st_store[hb], prep_state["ve"][hb])
            B.a_copy(phi, pstr)
            B.dma("scalar", dsth, phi, ctr=st_store[hb])
            B.wait("vector", B.ctr["scalar"])
            B.wait("vector", B.ctr["tensor"])
            B.wait("vector", st_store[hb], prep_state["ve"][hb])
            B.v_tt(plo, pstr, phi, ALU.subtract)
            B.psum_release(hp, "vector")
            B.wait("scalar", B.ctr["vector"])
            B.dma("scalar", dstl, plo, ctr=st_store[hb])
            prep_state["ve"][hb] = st_store[hb].n

        prep_steps2 = []

        def _mk_step(nm, os_, q, QI):
            tph, tpl = t_w[nm]
            nk = QI // 128
            return lambda: prep_slab(
                d_w[nm][os_ * 128:(os_ + 1) * 128, q:q + QI], nk,
                tph[os_, :, q:q + QI], tpl[os_, :, q:q + QI], late=True)

        for nm, O, I in (("Wk", H, H), ("Wv", H, H), ("Wgu", 2 * cfg.INTER, H)):
            tph, tpl = t_w[nm]
            for os_ in range(O // 128):
                q = 0
                while q < I:
                    QI = min(QP, I - q)
                    nk = QI // 128
                    prep_slab(d_w[nm][os_ * 128:(os_ + 1) * 128, q:q + QI], nk,
                              tph[os_, :, q:q + QI], tpl[os_, :, q:q + QI])
                    q += QI
        for nm, O, I in (("Wr", H, H), ("Wo", H, H)):
            for os_ in range(O // 128):
                q = 0
                while q < I:
                    QI = min(QP, I - q)
                    prep_steps2.append(_mk_step(nm, os_, q, QI))
                    q += QI
        for os_ in range(G):
            q = 0
            step = min(QP, ITB * 128)
            while q < cfg.INTER:
                QI = min(step, cfg.INTER - q)
                nk = QI // 128
                blk = q // (ITB * 128)
                c0 = q % (ITB * 128)
                prep_slab(d_w["Wd"][os_ * 128:(os_ + 1) * 128, q:q + QI], nk,
                          t_wd[0][blk, os_, :, c0:c0 + QI],
                          t_wd[1][blk, os_, :, c0:c0 + QI])
                q += QI
        B.barrier()

        for go in mod_steps:
            go()
        mod_steps = []
        B.wait("sync", st_wmst)
        wmback = P4.ap()[0:6 * G, 1024:1152]
        st_wmb = B.stream("wmb")
        B.dma("sync", wmback, d_modsc, ctr=st_wmb)
        B.wait("tensor", st_wmb)
        B.psum_acquire("A")
        B.mm(ps_big.ap()[:, 0:6 * G], wmback, eye.ap()[0:6 * G, 0:6 * G],
             start=True, stop=True, inc=True)
        B.wait("vector", B.ctr["tensor"])
        B.v_copy(modc.ap(), ps_big.ap()[:, 0:6 * G])
        B.psum_release("A", "vector")
        B.barrier()
        sa_c = modc.ap()[:, 0 * G:1 * G]
        ca_c = modc.ap()[:, 1 * G:2 * G]
        ga_c = modc.ap()[:, 2 * G:3 * G]
        sb_c = modc.ap()[:, 3 * G:4 * G]
        cb_c = modc.ap()[:, 4 * G:5 * G]
        gb_c = modc.ap()[:, 5 * G:6 * G]
        B.v_ts1(opca.ap(), ca_c, 1.0, ALU.add)
        B.v_ts1(opcb.ap(), cb_c, 1.0, ALU.add)
        for g_in, g_out in ((ga_c, gacl), (gb_c, gbcl)):
            t0 = scol.ap()[:, 0:G]
            t1 = scol.ap()[:, G:2 * G]
            t2 = scol.ap()[:, 2 * G:3 * G]
            B.v_ts1(t0, g_in, -1.0, ALU.mult)
            B.v_tt(t1, g_in, t0, ALU.max)
            B.v_ts1(t1, t1, 0.25, ALU.max)
            B.v_ts1(t2, g_in, 0.0, ALU.is_ge)
            B.v_ts2(t2, t2, 2.0, -1.0, ALU.mult, ALU.add)
            B.v_tt(g_out.ap(), t1, t2, ALU.mult)


        def ln_stats(src, cw, sqb, rows_eng="vector"):
            B.wait("scalar", B.ctr["vector"])
            B.wait("scalar", B.ctr["gpsimd"])
            B.wait("scalar", B.ctr["tensor"])
            B.a_act(sqb[:, 0:G * cw], src[:, 0:G * cw], AF.Square)
            B.wait("tensor", B.ctr["scalar"])
            B.psum_acquire("b0")
            B.psum_acquire("b1")
            for g in range(G):
                B.mm(ps_rows.ap()[0:1, 0:cw], ones_col.ap(), grp(src, g, cw),
                     start=(g == 0), stop=(g == G - 1), inc=(g == G - 1))
            for g in range(G):
                B.mm(ps_rows.ap()[0:1, 512:512 + cw], ones_col.ap(),
                     grp(sqb, g, cw), start=(g == 0), stop=(g == G - 1),
                     inc=(g == G - 1))
            E = rows_eng
            inv = 1.0 / H
            mu, vpe, s, ms, t1 = (row(0, cw), row(1, cw), row(2, cw),
                                  row(3, cw), row(4, cw))
            sst = row(5, cw)
            if E == "gpsimd":
                B.wait("scalar", B.ctr["tensor"])
                B.a_act(mu, ps_rows.ap()[0:1, 0:cw], AF.Identity, scale=inv)
                B.a_act(sst, ps_rows.ap()[0:1, 512:512 + cw], AF.Identity,
                        scale=inv)
                B.psum_release("b0", "scalar")
                B.psum_release("b1", "scalar")
                B.wait(E, B.ctr["scalar"])
                B.e_tt(E, t1, mu, mu, ALU.mult)
                B.e_tt(E, vpe, sst, t1, ALU.subtract)
            else:
                B.wait(E, B.ctr["tensor"])
                B.e_ts1(E, mu, ps_rows.ap()[0:1, 0:cw], inv, ALU.mult)
                B.e_tt(E, t1, mu, mu, ALU.mult)
                B.e_stt(E, vpe, ps_rows.ap()[0:1, 512:512 + cw], inv, t1,
                        ALU.mult, ALU.subtract)
                B.psum_release("b0", E)
                B.psum_release("b1", E)
            B.e_ts1(E, vpe, vpe, cfg.EPS, ALU.add)
            B.wait("scalar", B.ctr[E])
            B.a_act(t1, vpe, AF.Ln)
            B.a_act(s, t1, AF.Exp, scale=-0.5)
            B.wait(E, B.ctr["scalar"])
            for _ in range(2):
                B.e_tt(E, t1, s, s, ALU.mult)
                B.e_tt(E, t1, t1, vpe, ALU.mult)
                B.e_ts2(E, t1, t1, -0.5, 1.5, ALU.mult, ALU.add)
                B.e_tt(E, s, s, t1, ALU.mult)
            B.e_ts1(E, t1, mu, -1.0, ALU.mult)
            B.e_tt(E, ms, t1, s, ALU.mult)

        def ln_apply(src, dst, cw, scal_col, add_col, sqb, dst_cw=None,
                     dst_off=0, rows_eng="vector"):
            dcw = cw if dst_cw is None else dst_cw
            s, ms = row(2, cw), row(3, cw)
            B.wait("tensor", B.ctr[rows_eng])
            B.psum_acquire("b2")
            B.psum_acquire("b3")
            B.mm(ps_bc.ap()[:, 0:cw], ones_row.ap(), s, start=True, stop=True,
                 inc=False)
            B.mm(ps_bc.ap()[:, 512:512 + cw], ones_row.ap(), ms, start=True,
                 stop=True, inc=True)
            B.wait("vector", B.ctr["tensor"])
            s_b = ps_bc.ap()[:, 0:cw].unsqueeze(1).broadcast_to([128, G, cw])
            m_b = ps_bc.ap()[:, 512:512 + cw].unsqueeze(1).broadcast_to(
                [128, G, cw])
            src3 = src[:, 0:G * cw].rearrange("p (g c) -> p g c", c=cw)
            t2f = sqb[:, 0:G * cw]
            t23 = t2f.rearrange("p (g c) -> p g c", c=cw)
            B.v_tt(t23, src3, s_b, ALU.mult)
            B.v_tt(t23, t23, m_b, ALU.add)
            B.psum_release("b2", "vector")
            B.psum_release("b3", "vector")
            for g in range(G):
                B.v_stt(dst[:, g * dcw + dst_off:g * dcw + dst_off + cw],
                        grp(t2f, g, cw), scal_col[:, g:g + 1],
                        add_col[:, g:g + 1].broadcast_to([128, cw]),
                        ALU.mult, ALU.add)

        def hilo_split(src_f32, dsthi, dstlo, n):
            B.wait("scalar", B.ctr["vector"])
            B.a_copy(dsthi[:, 0:n], src_f32[:, 0:n])
            B.wait("gpsimd", B.ctr["scalar"])
            B.wait("gpsimd", B.ctr["vector"])
            B.g_tt(dstlo[:, 0:n], src_f32[:, 0:n], dsthi[:, 0:n], ALU.subtract)

        def stream_group(tpair, oi, kt, rhs_hi, rhs_lo, ps):
            nsub = (kt + KSUB - 1) // KSUB
            for s_ in range(nsub):
                k0 = s_ * KSUB
                k1 = min(kt, k0 + KSUB)
                buf = slabs.next()
                slabs.load(buf, k1 - k0,
                           tpair[0][oi][:, k0 * 128:k1 * 128],
                           tpair[1][oi][:, k0 * 128:k1 * 128])
                B.wait("tensor", slabs.st[buf])
                for k in range(k0, k1):
                    whi = slabs.tile(buf, k - k0, 0)
                    wlo = slabs.tile(buf, k - k0, 1)
                    rh = rhs_hi[:, k * C:(k + 1) * C]
                    rl = rhs_lo[:, k * C:(k + 1) * C]
                    B.mm(ps, whi, rh, start=(k == 0), stop=False)
                    B.mm(ps, whi, rl, start=False, stop=False)
                    B.mm(ps, wlo, rh, start=False, stop=(k == kt - 1),
                         inc=(k == k1 - 1))
                slabs.consumed(buf)

        SLOT4 = ("A", "B", "Cs", "Ds")

        def wstream(tp, n_out, kt, rhs_hi, rhs_lo, consume):
            for i in range(n_out):
                slot = SLOT4[i % 4]
                B.psum_acquire(slot)
                ps = MMSLOT[slot][:, 0:C]
                stream_group(tp, i, kt, rhs_hi, rhs_lo, ps)
                consume(i, ps, slot)

        n_out_dma = 0
        ALLC = NW + NMAIN

        def xhome(ci):
            return (P1 if ci % 2 == 0 else P6).ap()

        def khome(ci):
            return (P6 if ci % 2 == 0 else P1).ap()

        xtl = P10.ap()
        dbuf = P11.ap()
        st_xtm = B.stream("xtm")

        def front(nci):
            fwin = nci == 0
            fcw = C1 if fwin else C
            fxoff = 0 if nci == 0 else (C1 + (nci - 1) * C)
            xh = xhome(nci)

            def f0():
                B.wait("sync", B.ctr["tensor"])
                B.wait("sync", B.ctr["vector"])
                B.wait("sync", B.ctr["scalar"])
                dstx = xh[:, 0:G * fcw].rearrange("p (g c) -> p g c", c=fcw)
                srcx = d_x[:, fxoff:fxoff + fcw].rearrange(
                    "(g p) t -> p g t", p=128)
                B.dma("sync", dstx, srcx, ctr=st_xtm)

            def f1():
                B.wait("scalar", st_xtm)
                B.wait("tensor", st_xtm)
                B.wait("gpsimd", st_xtm)
                ln_stats(xh, fcw, dbuf, rows_eng="gpsimd")

            def f2():
                ln_apply(xh, h_fm, fcw, opca.ap(), sa_c, dbuf,
                         rows_eng="gpsimd")
                if nci != 0:
                    B.v_copy(col3(xtl, C1), xbcar.ap().unsqueeze(2))

            def f3():
                ln_stats(h_fm, fcw, dbuf, rows_eng="gpsimd")
                ln_apply(h_fm, xtl, fcw, cols["lnw_col"].ap(),
                         cols["lnb_col"].ap(), dbuf, dst_cw=C1,
                         dst_off=0 if nci == 0 else 1, rows_eng="gpsimd")

            def f4():
                d3 = dbuf[:, 0:GC].rearrange("p (g c) -> p g c", c=C)
                xt3 = xtl[:, 0:GC1].rearrange("p (g c) -> p g c", c=C1)
                B.v_tt(d3, xt3[:, :, 1:C1], xt3[:, :, 0:C], ALU.subtract)
                B.v_copy(xbcar.ap().unsqueeze(2), xt3[:, :, C:C1])
                xf = ekbuf[:, 2 * C:3 * C]
                xkhi = PA.ap()[:, 0:GC]
                xklo = PA.ap()[:, GC:2 * GC]
                B.wait("vector", B.cout)
                B.wait("vector", B.ctr["scalar"])
                B.wait("vector", B.ctr["gpsimd"])
                for g in range(G):
                    B.v_stt(xf, dbuf[:, g * C:(g + 1) * C],
                            cols["mixk_col"].ap()[:, g:g + 1],
                            xtl[:, g * C1:g * C1 + C], ALU.mult, ALU.add)
                    B.wait("scalar", B.ctr["vector"])
                    B.a_copy(xkhi[:, g * C:(g + 1) * C], xf)
                    B.wait("gpsimd", B.ctr["scalar"])
                    B.wait("gpsimd", B.ctr["vector"])
                    B.g_tt(xklo[:, g * C:(g + 1) * C], xf,
                           xkhi[:, g * C:(g + 1) * C], ALU.subtract)
                    B.wait("vector", B.ctr["gpsimd"])

            return [f0, f1, f2, f3, f4]

        def run_phases(ph):
            for f in ph:
                f()

        B.barrier()
        run_phases(front(0))

        for ci in range(ALLC):
            window = ci < NW
            xh = xhome(ci)
            kh = khome(ci)
            rwhi = PA.ap()[:, 0:GC]
            rwlo = PA.ap()[:, GC:2 * GC]
            pend = front(ci + 1) if ci + 1 < ALLC else []

            def consume_to(dst):
                def consume(oi, ps, slot):
                    B.wait("scalar", B.ctr["tensor"])
                    B.a_copy(grp(dst, oi, C), ps)
                    B.psum_release(slot, "scalar")
                return consume

            B.wait("tensor", B.ctr["scalar"])
            B.wait("tensor", B.ctr["gpsimd"])
            B.wait("scalar", B.ctr["vector"])
            wstream(t_w["Wk"], G, KTH, PA.ap()[:, 0:GC], PA.ap()[:, GC:2 * GC],
                    consume_to(kh))
            if window:
                for _ in range(11):
                    if prep_steps2:
                        prep_steps2.pop(0)()

            def mix_into(mixnm):
                B.wait("vector", B.cout)
                B.wait("vector", B.ctr["scalar"])
                B.wait("vector", B.ctr["gpsimd"])
                for g in range(G):
                    B.v_stt(grp(ekbuf, g, C), dbuf[:, g * C:(g + 1) * C],
                            cols[mixnm].ap()[:, g:g + 1],
                            xtl[:, g * C1:g * C1 + C], ALU.mult, ALU.add)
                hilo_split(ekbuf, PA.ap()[:, 0:GC], PA.ap()[:, GC:2 * GC], GC)
                B.wait("tensor", B.ctr["scalar"])
                B.wait("tensor", B.ctr["gpsimd"])

            mix_into("mixv_col")
            wstream(t_w["Wv"], G, KTH, PA.ap()[:, 0:GC], PA.ap()[:, GC:2 * GC],
                    consume_to(vbuf))
            if window:
                for _ in range(11):
                    if prep_steps2:
                        prep_steps2.pop(0)()
            act_after_kv = B.ctr["scalar"].n
            if not window:
                mix_into("mixr_col")

            B.wait("vector", B.ctr["scalar"], act_after_kv)
            km = scol.ap()[:, 0:G]
            for g in range(G):
                B.v_rmax(km[:, g:g + 1], grp(kh, g, C))
            kap_new = scol.ap()[:, G:2 * G]
            resc = scol.ap()[:, 2 * G:3 * G]
            resc2 = scol.ap()[:, 3 * G:4 * G]
            B.v_tt(kap_new, kappa.ap(), km, ALU.max)
            B.v_tt(resc, kappa.ap(), kap_new, ALU.subtract)
            B.wait("scalar", B.ctr["vector"])
            B.a_act(resc2, resc, AF.Exp)
            act_kappa = B.ctr["scalar"].n
            B.wait("vector", B.ctr["scalar"], act_kappa)
            B.v_tt(ncar.ap(), ncar.ap(), resc2, ALU.mult)
            B.v_tt(dcar.ap(), dcar.ap(), resc2, ALU.mult)
            B.v_copy(kappa.ap(), kap_new)
            B.v_ts1(negkap.ap(), kap_new, -1.0, ALU.mult)
            B.wait("scalar", B.ctr["vector"])
            B.wait("scalar", B.cout)
            for g in range(G):
                B.a_act(grp(sq, g, C), grp(kh, g, C), AF.Exp,
                        bias=negkap.ap()[:, g:g + 1])
            act_ek = B.ctr["scalar"].n

            if not window:
                def consume_r(oi, ps, slot):
                    B.wait("scalar", B.ctr["tensor"])
                    B.a_copy(grp(sigr, oi, C), ps)
                    B.psum_release(slot, "scalar")

                wstream(t_w["Wr"], G, KTH, PA.ap()[:, 0:GC],
                        PA.ap()[:, GC:2 * GC], consume_r)

            ekb = sq
            bb = ekbuf
            B.wait("vector", B.ctr["scalar"], act_ek)
            B.wait("vector", B.ctr["gpsimd"])
            B.v_tt(bb[:, 0:GC], ekb[:, 0:GC], vbuf[:, 0:GC], ALU.mult)
            N3 = xtl
            D3 = dbuf
            B.v_copy(col3(N3, C1), ncar.ap().unsqueeze(2))
            B.v_copy(col3(D3, C1), dcar.ap().unsqueeze(2))
            for g in range(G):
                lam_b = lam.ap()[:, g:g + 1].broadcast_to([128, C])
                B.v_scan(N3[:, g * C1 + 1:(g + 1) * C1], lam_b,
                         grp(bb, g, C), ncar.ap()[:, g:g + 1],
                         ALU.mult, ALU.add)
                B.v_scan(D3[:, g * C1 + 1:(g + 1) * C1], lam_b,
                         grp(ekb, g, C), dcar.ap()[:, g:g + 1],
                         ALU.mult, ALU.add)
            nlast = N3[:, 0:GC1].rearrange("p (g c) -> p g c", c=C1)[:, :, C:C1]
            dlast = D3[:, 0:GC1].rearrange("p (g c) -> p g c", c=C1)[:, :, C:C1]
            B.v_copy(ncar.ap().unsqueeze(2), nlast)
            B.v_copy(dcar.ap().unsqueeze(2), dlast)

            if window and ci == NW - 1:
                cb_b = cmask.ap().broadcast_to([128, G])
                B.v_tt(ncar.ap(), ncar.ap(), cb_b, ALU.mult)
                B.v_tt(dcar.ap(), dcar.ap(), cb_b, ALU.mult)
                B.v_tt(xbcar.ap(), xbcar.ap(), cb_b, ALU.mult)
            if window:
                if ci == NW - 1:
                    while prep_steps2:
                        prep_steps2.pop(0)()
                    B.wait("sync", st_store[0])
                    B.wait("sync", st_store[1])
                else:
                    for _ in range(11):
                        if prep_steps2:
                            prep_steps2.pop(0)()
                run_phases(pend)
                continue

            dens_all = sq[:, 0:GC]
            ln_all = vbuf[:, 0:GC]
            r0_all = vbuf[:, 0:GC]
            nums_all = ekbuf[:, 0:GC]
            for g in range(G):
                dpr = D3[:, g * C1:g * C1 + C]
                B.v_stt(grp(dens_all, g, C), grp(ekb, g, C),
                        expu.ap()[:, g:g + 1], dpr, ALU.mult, ALU.add)
            for g in range(G):
                npr = N3[:, g * C1:g * C1 + C]
                B.v_stt(grp(nums_all, g, C), grp(bb, g, C),
                        expu.ap()[:, g:g + 1], npr, ALU.mult, ALU.add)
            B.wait("scalar", B.ctr["vector"])
            B.a_act(ln_all, dens_all, AF.Ln)
            B.a_act(r0_all, ln_all, AF.Exp, scale=-1.0)
            sig_all = sigr[:, 0:GC]
            B.a_act(sig_all, sigr[:, 0:GC], AF.Sigmoid)
            B.wait("vector", B.ctr["scalar"])
            xy = dens_all
            B.v_tt(xy, dens_all, r0_all, ALU.mult)
            B.v_ts2(xy, xy, -1.0, 2.0, ALU.mult, ALU.add)
            B.v_tt(xy, xy, r0_all, ALU.mult)
            B.v_tt(nums_all, nums_all, xy, ALU.mult)
            B.v_tt(nums_all, nums_all, sig_all, ALU.mult)
            B.wait("scalar", B.ctr["vector"])
            B.a_copy(rwhi, nums_all)
            B.wait("gpsimd", B.ctr["scalar"])
            B.wait("gpsimd", B.ctr["vector"])
            B.g_tt(rwlo, nums_all, rwhi, ALU.subtract)

            B.wait("vector", B.ctr["scalar"])
            B.wait("vector", B.ctr["gpsimd"])
            for g in range(G):
                B.v_stt(grp(ekbuf, g, C), grp(h_fm, g, C),
                        gacl.ap()[:, g:g + 1], grp(xh, g, C),
                        ALU.mult, ALU.add)

            def consume_wo(oi, ps, slot):
                B.wait("vector", B.ctr["tensor"])
                B.v_stt(grp(xh, oi, C), ps, gacl.ap()[:, oi:oi + 1],
                        grp(ekbuf, oi, C), ALU.mult, ALU.add)
                B.psum_release(slot, "vector")

            B.wait("tensor", B.ctr["vector"])
            B.wait("tensor", B.ctr["scalar"])
            B.wait("tensor", B.ctr["gpsimd"])
            B.wait("vector", B.ctr["gpsimd"])
            wstream(t_w["Wo"], G, KTH, rwhi, rwlo, consume_wo)

            ln_stats(xh, C, sq)
            ln_apply(xh, vbuf, C, opcb.ap(), sb_c, sq)
            h2hi = PA.ap()[:, 0:GC]
            h2lo = PA.ap()[:, GC:2 * GC]
            hilo_split(vbuf, h2hi, h2lo, GC)
            B.wait("tensor", B.ctr["scalar"])
            B.wait("tensor", B.ctr["gpsimd"])

            acc = P4.ap()
            ahi = APR.ap()[:, 0:ITB * C]
            alo = APR.ap()[:, ITB * C:2 * ITB * C]
            af32 = vbuf
            if pend:
                pend.pop(0)()
            for blk in range(NBLK):
                for it in range(ITB):
                    og = blk * ITB + it
                    ou = GI + og
                    sg_slot = "A" if it % 2 == 0 else "Cs"
                    su_slot = "B" if it % 2 == 0 else "Ds"
                    B.psum_acquire(sg_slot)
                    psg = MMSLOT[sg_slot][:, 0:C]
                    stream_group(t_w["Wgu"], og, KTH, h2hi, h2lo, psg)
                    B.psum_acquire(su_slot)
                    psu = MMSLOT[su_slot][:, 0:C]
                    stream_group(t_w["Wgu"], ou, KTH, h2hi, h2lo, psu)
                    B.wait("scalar", B.ctr["tensor"])
                    sgb = (ekbuf[:, 0:C] if it % 2 == 0
                           else ekbuf[:, 3 * C:4 * C])
                    tb = (ekbuf[:, C:2 * C] if it % 2 == 0
                          else ekbuf[:, 4 * C:5 * C])
                    B.a_act(sgb, psg, AF.Sigmoid)
                    B.wait("vector", B.ctr["scalar"])
                    B.wait("vector", B.ctr["tensor"])
                    B.v_tt(tb, sgb, psu, ALU.mult)
                    B.psum_release(su_slot, "vector")
                    B.v_tt(grp(af32, it, C), tb, psg, ALU.mult)
                    B.psum_release(sg_slot, "vector")
                    B.wait("scalar", B.ctr["vector"])
                    B.a_copy(ahi[:, it * C:(it + 1) * C], grp(af32, it, C))
                    B.wait("gpsimd", B.ctr["scalar"])
                    B.wait("gpsimd", B.ctr["vector"])
                    B.g_tt(alo[:, it * C:(it + 1) * C], grp(af32, it, C),
                           ahi[:, it * C:(it + 1) * C], ALU.subtract)
                act_split = B.ctr["scalar"].n
                gp_split = B.ctr["gpsimd"].n
                if pend:
                    pend.pop(0)()
                B.wait("tensor", B.ctr["scalar"], act_split)
                B.wait("tensor", B.ctr["gpsimd"], gp_split)
                for ht in range(G):
                    slot = SLOT4[ht % 4]
                    B.psum_acquire(slot)
                    ps = MMSLOT[slot][:, 0:C]
                    stream_group((t_wd[0][blk], t_wd[1][blk]), ht, ITB,
                                 ahi, alo, ps)
                    if blk == 0:
                        B.wait("scalar", B.ctr["tensor"])
                        B.a_copy(grp(acc, ht, C), ps)
                        B.psum_release(slot, "scalar")
                    else:
                        B.wait("vector", B.ctr["tensor"])
                        B.wait("vector", B.ctr["scalar"])
                        B.v_tt(grp(acc, ht, C), ps, grp(acc, ht, C), ALU.add)
                        B.psum_release(slot, "vector")

            run_phases(pend)
            B.wait("vector", B.ctr["scalar"])
            B.wait("vector", B.ctr["gpsimd"])
            for g in range(G):
                B.v_stt(grp(sq, g, C), grp(acc, g, C),
                        gbcl.ap()[:, g:g + 1], grp(xh, g, C),
                        ALU.mult, ALU.add)
            mci = ci - NW
            B.wait("sync", B.ctr["vector"])
            dsto = d_out[:, mci * C:(mci + 1) * C].rearrange(
                "(g p) t -> p g t", p=128)
            srco = sq[:, 0:GC].rearrange("p (g c) -> p g c", c=C)
            B.dma("sync", dsto, srco, ctr=B.cout)
            n_out_dma += 1

        B.emit("sync", lambda e, s=B.cout.sem, v=16 * n_out_dma:
               e.wait_ge(s, v))

        block = ent(nc.Block())
        for engname in ENGS:
            seq = B.prog[engname]
            getattr(block, engname)(lambda e, seq=seq: [f(e) for f in seq])

    return nc



def _col(v, G):
    return np.ascontiguousarray(
        np.asarray(v, np.float32).reshape(G, 128).T).astype(np.float32)


def make_in_maps(cfg, x_full, weights, vecs):
    W, G = cfg.W, cfg.G
    in_maps = []
    for i in range(cfg.NC):
        t0 = i * cfg.TLOC
        if i == 0:
            xw = np.concatenate([x_full[0:W + 1], x_full[0:cfg.TLOC]], axis=0)
        else:
            assert t0 - W - 1 >= 0
            xw = x_full[t0 - W - 1:t0 + cfg.TLOC]
        m = {"x": np.ascontiguousarray(xw.T, dtype=np.float32)}
        for nm in ("Wk", "Wv", "Wr", "Wo", "Wgu", "Wd", "wmod"):
            m[nm] = weights[nm]
        m["c_col"] = vecs["c_col"]
        for nm in ("lnw", "lnb", "mixk", "mixv", "mixr", "td", "u"):
            m[nm + "_col"] = _col(vecs[nm], G)
        m["cmask"] = np.full((128, 1), 0.0 if i == 0 else 1.0, np.float32)
        m["eye"] = np.eye(128, dtype=np.float32)
        in_maps.append(m)
    return in_maps


def kernel(L0_attn_hidden_state, L0_attn_ln_w, L0_attn_ln_b, L0_attn_Wk,
           L0_attn_Wv, L0_attn_Wr, L0_attn_Wo, L0_attn_mix_k, L0_attn_mix_v,
           L0_attn_mix_r, L0_attn_time_decay, L0_attn_current_token_weight,
           L0_ffn_w_gate_up, L0_ffn_w_down, L0_c, L0_wmod):
    cfg = Cfg()
    x_full = np.ascontiguousarray(
        np.asarray(L0_attn_hidden_state, np.float32).reshape(cfg.T, cfg.H))
    weights = {
        "Wk": np.ascontiguousarray(np.asarray(L0_attn_Wk, np.float32)),
        "Wv": np.ascontiguousarray(np.asarray(L0_attn_Wv, np.float32)),
        "Wr": np.ascontiguousarray(np.asarray(L0_attn_Wr, np.float32)),
        "Wo": np.ascontiguousarray(np.asarray(L0_attn_Wo, np.float32)),
        "Wgu": np.ascontiguousarray(np.asarray(L0_ffn_w_gate_up, np.float32)),
        "Wd": np.ascontiguousarray(np.asarray(L0_ffn_w_down, np.float32)),
        "wmod": np.ascontiguousarray(np.asarray(L0_wmod, np.float32)),
    }
    vecs = {
        "lnw": L0_attn_ln_w, "lnb": L0_attn_ln_b, "mixk": L0_attn_mix_k,
        "mixv": L0_attn_mix_v, "mixr": L0_attn_mix_r,
        "td": L0_attn_time_decay, "u": L0_attn_current_token_weight,
        "c_col": _col(np.asarray(L0_c, np.float32).reshape(cfg.H), cfg.G),
    }
    in_maps = make_in_maps(cfg, x_full, weights, vecs)
    nc = build_nc(cfg)
    res = run_bass_kernel_spmd(nc, in_maps, core_ids=list(range(cfg.NC)))
    shards = [np.asarray(res.results[i]["out"]).T for i in range(cfg.NC)]
    out = np.concatenate(shards, axis=0).reshape(1, cfg.T, cfg.H)
    return out.astype(np.float32)


# version 20
# speedup vs baseline: 1.0608x; 1.0608x over previous
import numpy as np
from contextlib import ExitStack

import concourse.bass as bass
import concourse.mybir as mybir
from concourse.bass_utils import run_bass_kernel_spmd

F32 = mybir.dt.float32
BF16 = mybir.dt.bfloat16
ALU = mybir.AluOpType
AF = mybir.ActivationFunctionType
AX = mybir.AxisListType


class Cfg:
    def __init__(self, H=2048, T=16384, INTER=8192, NC=8, C=256, NW=2,
                 EPS=1e-5):
        self.H, self.T, self.INTER, self.NC, self.C, self.NW, self.EPS = (
            H, T, INTER, NC, C, NW, EPS)
        self.G = H // 128
        self.GI = INTER // 128
        self.TLOC = T // NC
        self.NMAIN = self.TLOC // C
        self.W = NW * C
        self.NBLK = max(1, INTER // 2048)
        self.ITB = self.GI // self.NBLK
        assert H % 128 == 0 and INTER % 128 == 0 and self.TLOC % C == 0
        assert C % 128 == 0 and self.GI % self.NBLK == 0


ENGS = ("sync", "tensor", "vector", "scalar", "gpsimd")


class Ctr:
    def __init__(self, sem):
        self.sem = sem
        self.n = 0


class Builder:
    def __init__(self, nc, es, cfg):
        self.nc, self.es, self.cfg = nc, es, cfg
        self.prog = {e: [] for e in ENGS}
        self.ctr = {e: Ctr(es.enter_context(nc.semaphore(f"c_{e}")))
                    for e in ENGS}
        self.cout = Ctr(es.enter_context(nc.semaphore("c_out")))
        self.dirty = {e: set() for e in ENGS}
        self.touched = {e: set() for e in ENGS}
        self.waited = {}
        self.pguard = {}
        self.streams = {}

    def stream(self, name):
        if name not in self.streams:
            self.streams[name] = Ctr(
                self.es.enter_context(self.nc.semaphore("st_" + name)))
        return self.streams[name]

    def emit(self, eng, f):
        self.prog[eng].append(f)

    def wait(self, eng, ctr, val=None):
        v = ctr.n if val is None else val
        if v <= 0:
            return
        key = (eng, ctr.sem.name)
        if self.waited.get(key, -1) >= v:
            return
        self.waited[key] = v
        self.emit(eng, lambda e, s=ctr.sem, v=v: e.wait_ge(s, v))

    def wait_all(self, eng):
        for name in ENGS:
            if name != eng:
                self.wait(eng, self.ctr[name])

    def barrier(self):
        for e in ENGS:
            self.wait_all(e)
            self.wait(e, self.cout)
            for s in self.streams.values():
                self.wait(e, s)

    @staticmethod
    def _names(aps):
        out = set()
        for a in aps:
            if a is None or isinstance(a, (int, float)):
                continue
            out.add(a.tensor.name)
        return out

    def _predrain(self, eng, outs, ins):
        innames = self._names(ins)
        outnames = self._names(outs)
        if (innames & self.dirty[eng]) or (outnames & self.touched[eng]):
            self.emit(eng, lambda e: e.drain())
            self.dirty[eng] = set()
            self.touched[eng] = set()
        self.dirty[eng] |= outnames
        self.touched[eng] |= outnames | innames

    def op(self, eng, outs, ins, f, inc=True):
        self._predrain(eng, outs, ins)
        c = self.ctr[eng]
        if inc:
            c.n += 1
            self.emit(eng, lambda e, f=f, s=c.sem: f(e).then_inc(s, 1))
        else:
            self.emit(eng, lambda e, f=f: f(e))

    def dma(self, eng, out, in_, ctr=None):
        if eng in ("vector", "scalar", "gpsimd", "tensor"):
            innames = self._names([in_])
            if innames & self.dirty[eng]:
                self.emit(eng, lambda e: e.drain())
                self.dirty[eng] = set()
                self.touched[eng] = set()
        c = ctr if ctr is not None else self.ctr[eng]
        c.n += 16
        self.emit(eng, lambda e, o=out, i=in_, s=c.sem:
                  e.dma_start(out=o, in_=i).then_inc(s, 16))

    PSUM_CONFLICTS = {
        "TR": ("TR", "A", "B", "Cs", "Ds"),
        "A": ("A", "TR"), "B": ("B", "TR"),
        "Cs": ("Cs", "TR"), "Ds": ("Ds", "TR"),
        "PR0": ("PR0", "b0", "b1"), "PR1": ("PR1", "b2", "b3"),
        "b0": ("b0", "PR0"), "b1": ("b1", "PR0"),
        "b2": ("b2", "PR1"), "b3": ("b3", "PR1"),
    }

    def psum_acquire(self, key, eng="tensor"):
        for k in self.PSUM_CONFLICTS.get(key, (key,)):
            g = self.pguard.get(k)
            if g is not None:
                ctr, val = g
                self.wait(eng, ctr, val)

    def psum_release(self, key, eng):
        self.pguard[key] = (self.ctr[eng], self.ctr[eng].n)

    def v_tt(self, out, a, b, alu, inc=True):
        self.op("vector", [out], [a, b],
                lambda e: e.tensor_tensor(out=out, in0=a, in1=b, op=alu), inc)

    def v_ts1(self, out, a, s1, op0, inc=True):
        sc = None if isinstance(s1, float) else s1
        self.op("vector", [out], [a, sc],
                lambda e: e.tensor_single_scalar(out, a, s1, op0), inc)

    def v_ts2(self, out, a, s1, s2, op0, op1, inc=True):
        i2 = [a] + [x for x in (s1, s2) if not isinstance(x, float)]
        self.op("vector", [out], i2,
                lambda e: e.tensor_scalar(out, a, s1, s2, op0, op1), inc)

    def v_stt(self, out, in0, scal, in1, op0, op1, inc=True):
        self.op("vector", [out], [in0, scal, in1],
                lambda e: e.scalar_tensor_tensor(out, in0, scal, in1, op0, op1),
                inc)

    def v_copy(self, out, a, inc=True):
        self.op("vector", [out], [a], lambda e: e.tensor_copy(out, a), inc)

    def v_scan(self, out, d0, d1, init, op0, op1, inc=True):
        self.op("vector", [out], [d0, d1, init],
                lambda e: e.tensor_tensor_scan(out=out, data0=d0, data1=d1,
                                               initial=init, op0=op0, op1=op1),
                inc)

    def v_rmax(self, out, a, inc=True):
        self.op("vector", [out], [a],
                lambda e: e.reduce_max(out, a, axis=AX.X), inc)

    def v_memset(self, out, val, inc=True):
        self.op("vector", [out], [], lambda e: e.memset(out, val), inc)

    def v_recip(self, out, a, inc=True):
        self.op("vector", [out], [a], lambda e: e.reciprocal(out, a), inc)

    def a_act(self, out, a, fn, bias=0.0, scale=1.0, inc=True):
        ins = [a] + [x for x in (bias, scale) if not isinstance(x, float)]
        self.op("scalar", [out], ins,
                lambda e: e.activation(out=out, in_=a, func=fn, bias=bias,
                                       scale=scale), inc)

    def a_copy(self, out, a, inc=True):
        self.op("scalar", [out], [a], lambda e: e.copy(out=out, in_=a), inc)

    def g_tt(self, out, a, b, alu, inc=True):
        self.op("gpsimd", [out], [a, b],
                lambda e: e.tensor_tensor(out=out, in0=a, in1=b, op=alu), inc)

    def e_tt(self, eng, out, a, b, alu, inc=True):
        self.op(eng, [out], [a, b],
                lambda e: e.tensor_tensor(out=out, in0=a, in1=b, op=alu), inc)

    def e_ts1(self, eng, out, a, s1, op0, inc=True):
        sc = None if isinstance(s1, float) else s1
        self.op(eng, [out], [a, sc],
                lambda e: e.tensor_single_scalar(out, a, s1, op0), inc)

    def e_ts2(self, eng, out, a, s1, s2, op0, op1, inc=True):
        i2 = [a] + [x for x in (s1, s2) if not isinstance(x, float)]
        self.op(eng, [out], i2,
                lambda e: e.tensor_scalar(out, a, s1, s2, op0, op1), inc)

    def e_stt(self, eng, out, in0, scal, in1, op0, op1, inc=True):
        self.op(eng, [out], [in0, scal, in1],
                lambda e: e.scalar_tensor_tensor(out, in0, scal, in1, op0,
                                                 op1), inc)

    def e_copy(self, eng, out, a, inc=True):
        self.op(eng, [out], [a], lambda e: e.tensor_copy(out, a), inc)

    def mm(self, out, lhsT, rhs, start, stop, inc=False):
        c = self.ctr["tensor"]
        if inc:
            c.n += 1
            self.emit("tensor", lambda e, o=out, l=lhsT, r=rhs, st=start,
                      sp=stop, s=c.sem:
                      e.matmul(o, l, r, start=st, stop=sp).then_inc(s, 1))
        else:
            self.emit("tensor",
                      lambda e, o=out, l=lhsT, r=rhs, st=start, sp=stop:
                      e.matmul(o, l, r, start=st, stop=sp))


class SlabStream:

    NB = 4

    def __init__(self, B, slab_ap_bf16, unit_cols):
        self.B = B
        self.ap = slab_ap_bf16
        self.uc = unit_cols
        self.pe_done = [0] * self.NB
        self.kt = [0] * self.NB
        self.st = [B.stream(f"slab{i}") for i in range(self.NB)]
        self.rr = 0

    def next(self):
        b = self.rr % self.NB
        self.rr += 1
        return b

    def load(self, buf, kt, src):
        B = self.B
        self.kt[buf] = kt
        B.wait("sync", B.ctr["tensor"], self.pe_done[buf])
        base = buf * self.uc
        dst = self.ap[:, base:base + 2 * kt * 128].rearrange(
            "p (s q) -> p s q", s=2)
        B.dma("sync", dst, src, ctr=self.st[buf])

    def tile(self, buf, k, hilo):
        o = buf * self.uc + hilo * self.kt[buf] * 128 + k * 128
        return self.ap[:, o:o + 128]

    def consumed(self, buf):
        self.pe_done[buf] = self.B.ctr["tensor"].n


def build_nc(cfg: Cfg):
    H, G, C = cfg.H, cfg.G, cfg.C
    GI, NBLK, ITB = cfg.GI, cfg.NBLK, cfg.ITB
    NW, NMAIN, TLOC = cfg.NW, cfg.NMAIN, cfg.TLOC
    NTOK = NW * C + 1 + TLOC
    C1 = C + 1
    KTH = G
    SLABMAX = max(KTH, ITB) * 256

    nc = bass.Bass("TRN2", target_bir_lowering=False, debug=False)

    d_x = nc.dram_tensor("x", [H, NTOK], F32, kind="ExternalInput").ap()
    d_w = {}
    for nm, shp in (("Wk", [H, H]), ("Wv", [H, H]), ("Wr", [H, H]),
                    ("Wo", [H, H]), ("Wgu", [2 * cfg.INTER, H]),
                    ("Wd", [H, cfg.INTER]), ("wmod", [H, 6 * H])):
        d_w[nm] = nc.dram_tensor(nm, shp, F32, kind="ExternalInput").ap()
    d_cols = {}
    for nm in ("c_col", "lnw_col", "lnb_col", "mixk_col", "mixv_col",
               "mixr_col", "td_col", "u_col"):
        d_cols[nm] = nc.dram_tensor(nm, [128, G], F32, kind="ExternalInput").ap()
    d_cmask = nc.dram_tensor("cmask", [128, 1], F32, kind="ExternalInput").ap()
    d_eye = nc.dram_tensor("eye", [128, 128], F32, kind="ExternalInput").ap()
    d_out = nc.dram_tensor("out", [H, TLOC], F32, kind="ExternalOutput").ap()

    def tape(name, O, I):
        return nc.dram_tensor(f"{name}", [O // 128, 128, 2,
                                          (I // 128) * 128],
                              BF16, kind="Internal").ap()

    t_w = {
        "Wk": tape("t_wk", H, H), "Wv": tape("t_wv", H, H),
        "Wr": tape("t_wr", H, H), "Wo": tape("t_wo", H, H),
        "Wgu": tape("t_wgu", 2 * cfg.INTER, H),
    }
    t_wd = nc.dram_tensor("t_wd", [NBLK, G, 128, 2, ITB * 128], BF16,
                          kind="Internal").ap()
    d_modsc = nc.dram_tensor("modsc", [6 * G, 128], F32, kind="Internal").ap()

    es = ExitStack()
    with es:
        ent = es.enter_context

        def sb(name, cols, dt=F32):
            return ent(nc.sbuf_tensor(name, [128, cols], dt))

        GC = G * C
        GC1 = G * C1
        P1 = sb("P1", GC1)
        P2 = sb("P2", GC1)
        P3 = sb("P3", max(GC1, ITB * C, 10 * C))
        P4 = sb("P4", max(GC1, 1280))
        QP0 = min(1024, (max(GC, 2 * G * 128) // 2 // 128) * 128)
        PA = sb("PA", 2 * GC, BF16)
        APR = sb("APR", max(2 * ITB * C, 4 * QP0), BF16)
        P6 = sb("P6", max(GC, 2 * G * 128))
        P7 = sb("P7", max(GC, G * 128, ITB * C))
        P9 = sb("P9", max(GC, 5 * C))
        P10 = sb("P10", GC1)
        P11 = sb("P11", GC1)
        P13 = sb("P13", 8 * min(8, max(KTH, ITB)) * 128, BF16)

        cols = {}
        for nm in ("c_col", "lnw_col", "lnb_col", "mixk_col", "mixv_col",
                   "mixr_col", "td_col", "u_col"):
            cols[nm] = sb("s_" + nm, G)
        cmask = sb("cmaskb", 1)
        eye = sb("eyeb", 128)
        lam = sb("lam", G)
        expu = sb("expu", G)
        kappa = sb("kappa", G)
        negkap = sb("negkap", G)
        ncar = sb("ncar", G)
        dcar = sb("dcar", G)
        xbcar = sb("xbcar", G)
        modc = sb("modc", 6 * G)
        opca = sb("opca", G)
        opcb = sb("opcb", G)
        gacl = sb("gacl", G)
        gbcl = sb("gbcl", G)
        scol = sb("scol", 4 * G)
        ones_row = ent(nc.sbuf_tensor("ones_row", [1, 128], F32))
        ones_col = ent(nc.sbuf_tensor("ones_col", [128, 1], F32))
        rows = ent(nc.sbuf_tensor("rows", [1, 6 * C1], F32))

        def row(i, cw):
            return rows.ap()[0:1, i * C1:i * C1 + cw]


        ps_rows = ent(nc.psum_tensor("ps_rows", [128, 1024], F32))
        ps_bc = ent(nc.psum_tensor("ps_bc", [128, 1024], F32))
        ps_big = ent(nc.psum_tensor("ps_big", [128, 2048], F32))

        B = Builder(nc, es, cfg)

        x_fm = P1.ap()
        h_fm = P2.ap()
        xtld = P3.ap()
        sq = P4.ap()
        kbuf = P6.ap()
        vbuf = P7.ap()
        sigr = P3.ap()
        ekbuf = P9.ap()
        N3 = P10.ap()
        D3 = P11.ap()

        MMSLOT = {
            "A": ps_big.ap()[:, 0:512],
            "B": ps_big.ap()[:, 512:1024],
            "Cs": ps_big.ap()[:, 1024:1536],
            "Ds": ps_big.ap()[:, 1536:2048],
        }
        KSUB = min(8, max(KTH, ITB))
        slabs = SlabStream(B, P13.ap(), KSUB * 256)

        def grp(ap_, g, cw, width=None):
            w = cw if width is None else width
            return ap_[:, g * cw:g * cw + w]

        def col3(ap_, cw):
            return ap_[:, 0:G * cw].rearrange("p (g c) -> p g c",
                                              c=cw)[:, :, 0:1]

        st_cols = B.stream("cols")
        for nm in cols:
            B.dma("sync", cols[nm].ap(), d_cols[nm], ctr=st_cols)
        B.dma("sync", cmask.ap(), d_cmask, ctr=st_cols)
        B.dma("sync", eye.ap(), d_eye, ctr=st_cols)
        B.wait("vector", st_cols)
        B.wait("scalar", st_cols)
        B.a_act(scol.ap()[:, 0:G], cols["td_col"].ap(), AF.Exp)
        B.a_act(lam.ap(), scol.ap()[:, 0:G], AF.Exp, scale=-1.0)
        B.a_act(expu.ap(), cols["u_col"].ap(), AF.Exp)
        B.v_memset(kappa.ap(), -1e30)
        B.v_memset(ncar.ap(), 0.0)
        B.v_memset(dcar.ap(), 0.0)
        B.v_memset(xbcar.ap(), 0.0)
        B.v_memset(ones_row.ap(), 1.0)
        B.v_memset(ones_col.ap(), 1.0)

        B.a_act(scol.ap()[:, G:2 * G], cols["c_col"].ap(), AF.Sigmoid)
        B.wait("vector", B.ctr["scalar"])
        B.v_tt(scol.ap()[:, 2 * G:3 * G], cols["c_col"].ap(),
               scol.ap()[:, G:2 * G], ALU.mult)
        siluc = scol.ap()[:, 2 * G:3 * G]
        st_wm = [B.stream("wm0"), B.stream("wm1")]
        st_wmst = B.stream("wmst")
        wm_pe_done = [0, 0]
        mrow = rows.ap()[0:1, 0:512]
        NSL = 6 * H // 512
        mod_ps = [ps_bc.ap()[0:1, 0:512], ps_bc.ap()[0:1, 512:1024]]
        mod_key = ["b2", "b3"]

        def mod_step(ns):
            def go():
                B.psum_acquire(mod_key[ns % 2])
                ps = mod_ps[ns % 2]
                for kk in range(KTH):
                    half = kk % 2
                    wmtile = P4.ap()[:, half * 512:half * 512 + 512]
                    B.wait("sync", B.ctr["tensor"], wm_pe_done[half])
                    B.dma("sync", wmtile,
                          d_w["wmod"][kk * 128:(kk + 1) * 128,
                                      ns * 512:(ns + 1) * 512],
                          ctr=st_wm[half])
                    B.wait("tensor", st_wm[half])
                    B.mm(ps, siluc[:, kk:kk + 1], wmtile,
                         start=(kk == 0), stop=(kk == KTH - 1), inc=True)
                    wm_pe_done[half] = B.ctr["tensor"].n
                B.wait("scalar", st_wmst)
                B.wait("scalar", B.ctr["tensor"])
                B.a_copy(mrow, ps)
                B.psum_release(mod_key[ns % 2], "scalar")
                B.dma("scalar", d_modsc[4 * ns:4 * ns + 4, :].flatten(),
                      mrow, ctr=st_wmst)
            return go

        mod_steps = [mod_step(ns) for ns in range(NSL)]

        B.barrier()
        QP = QP0
        NPN = 4 if P2.shape[1] >= 2 * QP else 2
        st_pnat = [B.stream(f"pnat{j}") for j in range(4)]
        st_store = [B.stream("pstore0"), B.stream("pstore1")]
        prep_state = {"pe": [0, 0, 0, 0], "ve": [0, 0], "i": 0}
        PTR_PS = [ps_big.ap()[:, 0:1024], ps_big.ap()[:, 1024:2048],
                  ps_rows.ap()[:, 0:1024]]

        def prep_slab(src, nk, dsth, dstl, late=False):
            i = prep_state["i"]
            prep_state["i"] += 1
            if mod_steps and i % 12 == 8:
                mod_steps.pop(0)()
            hb = i % 2
            if late:
                hp = "PR" + str(i % 2)
                pstr_t = (ps_rows if i % 2 == 0 else ps_bc).ap()
                base = hb * 2 * QP
                pnat = P3.ap()[:, base:base + nk * 128]
                phi = APR.ap()[:, hb * QP:hb * QP + nk * 128]
                plo = APR.ap()[:, 2 * QP + hb * QP:2 * QP + hb * QP + nk * 128]
            else:
                hp = "PTR" + str(i % 3)
                pstr_t = PTR_PS[i % 3]
                hb4 = i % NPN
                pn_t = (P6 if hb4 < 2 else P2).ap()
                pnat = pn_t[:, (hb4 % 2) * QP:(hb4 % 2) * QP + nk * 128]
                phi = APR.ap()[:, hb * QP:hb * QP + nk * 128]
                plo = APR.ap()[:, 2 * QP + hb * QP:2 * QP + hb * QP + nk * 128]
            pstr = pstr_t[:, 0:nk * 128]
            pb = hb if late else (i % NPN)
            B.wait("sync", B.ctr["tensor"], prep_state["pe"][pb])
            B.dma("sync", pnat, src, ctr=st_pnat[pb])
            B.wait("tensor", st_pnat[pb])
            B.psum_acquire(hp)
            for k in range(nk):
                B.mm(pstr[:, k * 128:(k + 1) * 128],
                     pnat[:, k * 128:(k + 1) * 128], eye.ap(),
                     start=True, stop=True, inc=(k == nk - 1))
            prep_state["pe"][pb] = B.ctr["tensor"].n
            B.wait("scalar", B.ctr["tensor"])
            B.wait("scalar", st_store[hb], prep_state["ve"][hb])
            B.a_copy(phi, pstr)
            B.dma("scalar", dsth, phi, ctr=st_store[hb])
            B.wait("vector", B.ctr["scalar"])
            B.wait("vector", B.ctr["tensor"])
            B.wait("vector", st_store[hb], prep_state["ve"][hb])
            B.v_tt(plo, pstr, phi, ALU.subtract)
            B.psum_release(hp, "vector")
            B.wait("scalar", B.ctr["vector"])
            B.dma("scalar", dstl, plo, ctr=st_store[hb])
            prep_state["ve"][hb] = st_store[hb].n

        prep_steps2 = []

        def tview(tp, os_, s_, q, QI):
            return tp[os_, :, s_:s_ + 1, q:q + QI].rearrange(
                "p s i -> p (s i)")

        def _mk_step(nm, os_, q, QI):
            tp = t_w[nm]
            nk = QI // 128
            return lambda: prep_slab(
                d_w[nm][os_ * 128:(os_ + 1) * 128, q:q + QI], nk,
                tview(tp, os_, 0, q, QI), tview(tp, os_, 1, q, QI),
                late=True)

        for nm, O, I in (("Wk", H, H), ("Wv", H, H), ("Wgu", 2 * cfg.INTER, H)):
            tp = t_w[nm]
            for os_ in range(O // 128):
                q = 0
                while q < I:
                    QI = min(QP, I - q)
                    nk = QI // 128
                    prep_slab(d_w[nm][os_ * 128:(os_ + 1) * 128, q:q + QI], nk,
                              tview(tp, os_, 0, q, QI),
                              tview(tp, os_, 1, q, QI))
                    q += QI
        for nm, O, I in (("Wr", H, H), ("Wo", H, H)):
            for os_ in range(O // 128):
                q = 0
                while q < I:
                    QI = min(QP, I - q)
                    prep_steps2.append(_mk_step(nm, os_, q, QI))
                    q += QI
        for os_ in range(G):
            q = 0
            step = min(QP, ITB * 128)
            while q < cfg.INTER:
                QI = min(step, cfg.INTER - q)
                nk = QI // 128
                blk = q // (ITB * 128)
                c0 = q % (ITB * 128)
                prep_slab(d_w["Wd"][os_ * 128:(os_ + 1) * 128, q:q + QI], nk,
                          t_wd[blk, os_, :, 0:1, c0:c0 + QI].rearrange(
                              "p s i -> p (s i)"),
                          t_wd[blk, os_, :, 1:2, c0:c0 + QI].rearrange(
                              "p s i -> p (s i)"))
                q += QI
        B.barrier()

        for go in mod_steps:
            go()
        mod_steps = []
        B.wait("sync", st_wmst)
        wmback = P4.ap()[0:6 * G, 1024:1152]
        st_wmb = B.stream("wmb")
        B.dma("sync", wmback, d_modsc, ctr=st_wmb)
        B.wait("tensor", st_wmb)
        B.psum_acquire("A")
        B.mm(ps_big.ap()[:, 0:6 * G], wmback, eye.ap()[0:6 * G, 0:6 * G],
             start=True, stop=True, inc=True)
        B.wait("vector", B.ctr["tensor"])
        B.v_copy(modc.ap(), ps_big.ap()[:, 0:6 * G])
        B.psum_release("A", "vector")
        B.barrier()
        sa_c = modc.ap()[:, 0 * G:1 * G]
        ca_c = modc.ap()[:, 1 * G:2 * G]
        ga_c = modc.ap()[:, 2 * G:3 * G]
        sb_c = modc.ap()[:, 3 * G:4 * G]
        cb_c = modc.ap()[:, 4 * G:5 * G]
        gb_c = modc.ap()[:, 5 * G:6 * G]
        B.v_ts1(opca.ap(), ca_c, 1.0, ALU.add)
        B.v_ts1(opcb.ap(), cb_c, 1.0, ALU.add)
        for g_in, g_out in ((ga_c, gacl), (gb_c, gbcl)):
            t0 = scol.ap()[:, 0:G]
            t1 = scol.ap()[:, G:2 * G]
            t2 = scol.ap()[:, 2 * G:3 * G]
            B.v_ts1(t0, g_in, -1.0, ALU.mult)
            B.v_tt(t1, g_in, t0, ALU.max)
            B.v_ts1(t1, t1, 0.25, ALU.max)
            B.v_ts1(t2, g_in, 0.0, ALU.is_ge)
            B.v_ts2(t2, t2, 2.0, -1.0, ALU.mult, ALU.add)
            B.v_tt(g_out.ap(), t1, t2, ALU.mult)


        def ln_stats(src, cw, sqb, rows_eng="vector"):
            B.wait("scalar", B.ctr["vector"])
            B.wait("scalar", B.ctr["gpsimd"])
            B.wait("scalar", B.ctr["tensor"])
            B.a_act(sqb[:, 0:G * cw], src[:, 0:G * cw], AF.Square)
            B.wait("tensor", B.ctr["scalar"])
            B.psum_acquire("b0")
            B.psum_acquire("b1")
            for g in range(G):
                B.mm(ps_rows.ap()[0:1, 0:cw], ones_col.ap(), grp(src, g, cw),
                     start=(g == 0), stop=(g == G - 1), inc=(g == G - 1))
            for g in range(G):
                B.mm(ps_rows.ap()[0:1, 512:512 + cw], ones_col.ap(),
                     grp(sqb, g, cw), start=(g == 0), stop=(g == G - 1),
                     inc=(g == G - 1))
            E = rows_eng
            inv = 1.0 / H
            mu, vpe, s, ms, t1 = (row(0, cw), row(1, cw), row(2, cw),
                                  row(3, cw), row(4, cw))
            sst = row(5, cw)
            if E == "gpsimd":
                B.wait("scalar", B.ctr["tensor"])
                B.a_act(mu, ps_rows.ap()[0:1, 0:cw], AF.Identity, scale=inv)
                B.a_act(sst, ps_rows.ap()[0:1, 512:512 + cw], AF.Identity,
                        scale=inv)
                B.psum_release("b0", "scalar")
                B.psum_release("b1", "scalar")
                B.wait(E, B.ctr["scalar"])
                B.e_tt(E, t1, mu, mu, ALU.mult)
                B.e_tt(E, vpe, sst, t1, ALU.subtract)
            else:
                B.wait(E, B.ctr["tensor"])
                B.e_ts1(E, mu, ps_rows.ap()[0:1, 0:cw], inv, ALU.mult)
                B.e_tt(E, t1, mu, mu, ALU.mult)
                B.e_stt(E, vpe, ps_rows.ap()[0:1, 512:512 + cw], inv, t1,
                        ALU.mult, ALU.subtract)
                B.psum_release("b0", E)
                B.psum_release("b1", E)
            B.e_ts1(E, vpe, vpe, cfg.EPS, ALU.add)
            B.wait("scalar", B.ctr[E])
            B.a_act(t1, vpe, AF.Ln)
            B.a_act(s, t1, AF.Exp, scale=-0.5)
            B.wait(E, B.ctr["scalar"])
            for _ in range(2):
                B.e_tt(E, t1, s, s, ALU.mult)
                B.e_tt(E, t1, t1, vpe, ALU.mult)
                B.e_ts2(E, t1, t1, -0.5, 1.5, ALU.mult, ALU.add)
                B.e_tt(E, s, s, t1, ALU.mult)
            B.e_ts1(E, t1, mu, -1.0, ALU.mult)
            B.e_tt(E, ms, t1, s, ALU.mult)

        def ln_apply(src, dst, cw, scal_col, add_col, sqb, dst_cw=None,
                     dst_off=0, rows_eng="vector"):
            dcw = cw if dst_cw is None else dst_cw
            s, ms = row(2, cw), row(3, cw)
            B.wait("tensor", B.ctr[rows_eng])
            B.psum_acquire("b2")
            B.psum_acquire("b3")
            B.mm(ps_bc.ap()[:, 0:cw], ones_row.ap(), s, start=True, stop=True,
                 inc=False)
            B.mm(ps_bc.ap()[:, 512:512 + cw], ones_row.ap(), ms, start=True,
                 stop=True, inc=True)
            B.wait("vector", B.ctr["tensor"])
            s_b = ps_bc.ap()[:, 0:cw].unsqueeze(1).broadcast_to([128, G, cw])
            m_b = ps_bc.ap()[:, 512:512 + cw].unsqueeze(1).broadcast_to(
                [128, G, cw])
            src3 = src[:, 0:G * cw].rearrange("p (g c) -> p g c", c=cw)
            t2f = sqb[:, 0:G * cw]
            t23 = t2f.rearrange("p (g c) -> p g c", c=cw)
            B.v_tt(t23, src3, s_b, ALU.mult)
            B.v_tt(t23, t23, m_b, ALU.add)
            B.psum_release("b2", "vector")
            B.psum_release("b3", "vector")
            for g in range(G):
                B.v_stt(dst[:, g * dcw + dst_off:g * dcw + dst_off + cw],
                        grp(t2f, g, cw), scal_col[:, g:g + 1],
                        add_col[:, g:g + 1].broadcast_to([128, cw]),
                        ALU.mult, ALU.add)

        def hilo_split(src_f32, dsthi, dstlo, n):
            B.wait("scalar", B.ctr["vector"])
            B.a_copy(dsthi[:, 0:n], src_f32[:, 0:n])
            B.wait("gpsimd", B.ctr["scalar"])
            B.wait("gpsimd", B.ctr["vector"])
            B.g_tt(dstlo[:, 0:n], src_f32[:, 0:n], dsthi[:, 0:n], ALU.subtract)

        def stream_group(tpair, oi, kt, rhs_hi, rhs_lo, ps):
            nsub = (kt + KSUB - 1) // KSUB
            for s_ in range(nsub):
                k0 = s_ * KSUB
                k1 = min(kt, k0 + KSUB)
                buf = slabs.next()
                slabs.load(buf, k1 - k0,
                           tpair[oi, :, :, k0 * 128:k1 * 128])
                B.wait("tensor", slabs.st[buf])
                for k in range(k0, k1):
                    whi = slabs.tile(buf, k - k0, 0)
                    wlo = slabs.tile(buf, k - k0, 1)
                    rh = rhs_hi[:, k * C:(k + 1) * C]
                    rl = rhs_lo[:, k * C:(k + 1) * C]
                    B.mm(ps, whi, rh, start=(k == 0), stop=False)
                    B.mm(ps, whi, rl, start=False, stop=False)
                    B.mm(ps, wlo, rh, start=False, stop=(k == kt - 1),
                         inc=(k == k1 - 1))
                slabs.consumed(buf)

        SLOT4 = ("A", "B", "Cs", "Ds")

        def wstream(tp, n_out, kt, rhs_hi, rhs_lo, consume):
            for i in range(n_out):
                slot = SLOT4[i % 4]
                B.psum_acquire(slot)
                ps = MMSLOT[slot][:, 0:C]
                stream_group(tp, i, kt, rhs_hi, rhs_lo, ps)
                consume(i, ps, slot)

        n_out_dma = 0
        ALLC = NW + NMAIN

        def xhome(ci):
            return (P1 if ci % 2 == 0 else P6).ap()

        def khome(ci):
            return (P6 if ci % 2 == 0 else P1).ap()

        xtl = P10.ap()
        dbuf = P11.ap()
        st_xtm = B.stream("xtm")

        def front(nci):
            fwin = nci == 0
            fcw = C1 if fwin else C
            fxoff = 0 if nci == 0 else (C1 + (nci - 1) * C)
            xh = xhome(nci)

            def f0():
                B.wait("sync", B.ctr["tensor"])
                B.wait("sync", B.ctr["vector"])
                B.wait("sync", B.ctr["scalar"])
                dstx = xh[:, 0:G * fcw].rearrange("p (g c) -> p g c", c=fcw)
                srcx = d_x[:, fxoff:fxoff + fcw].rearrange(
                    "(g p) t -> p g t", p=128)
                B.dma("sync", dstx, srcx, ctr=st_xtm)

            def f1():
                B.wait("scalar", st_xtm)
                B.wait("tensor", st_xtm)
                B.wait("gpsimd", st_xtm)
                ln_stats(xh, fcw, dbuf, rows_eng="gpsimd")

            def f2():
                ln_apply(xh, h_fm, fcw, opca.ap(), sa_c, dbuf,
                         rows_eng="gpsimd")
                if nci != 0:
                    B.v_copy(col3(xtl, C1), xbcar.ap().unsqueeze(2))

            def f3():
                ln_stats(h_fm, fcw, dbuf, rows_eng="gpsimd")
                ln_apply(h_fm, xtl, fcw, cols["lnw_col"].ap(),
                         cols["lnb_col"].ap(), dbuf, dst_cw=C1,
                         dst_off=0 if nci == 0 else 1, rows_eng="gpsimd")

            def f4():
                d3 = dbuf[:, 0:GC].rearrange("p (g c) -> p g c", c=C)
                xt3 = xtl[:, 0:GC1].rearrange("p (g c) -> p g c", c=C1)
                B.v_tt(d3, xt3[:, :, 1:C1], xt3[:, :, 0:C], ALU.subtract)
                B.v_copy(xbcar.ap().unsqueeze(2), xt3[:, :, C:C1])
                xf = ekbuf[:, 2 * C:3 * C]
                xkhi = PA.ap()[:, 0:GC]
                xklo = PA.ap()[:, GC:2 * GC]
                B.wait("vector", B.cout)
                B.wait("vector", B.ctr["scalar"])
                B.wait("vector", B.ctr["gpsimd"])
                for g in range(G):
                    B.v_stt(xf, dbuf[:, g * C:(g + 1) * C],
                            cols["mixk_col"].ap()[:, g:g + 1],
                            xtl[:, g * C1:g * C1 + C], ALU.mult, ALU.add)
                    B.wait("scalar", B.ctr["vector"])
                    B.a_copy(xkhi[:, g * C:(g + 1) * C], xf)
                    B.wait("gpsimd", B.ctr["scalar"])
                    B.wait("gpsimd", B.ctr["vector"])
                    B.g_tt(xklo[:, g * C:(g + 1) * C], xf,
                           xkhi[:, g * C:(g + 1) * C], ALU.subtract)
                    B.wait("vector", B.ctr["gpsimd"])

            return [f0, f1, f2, f3, f4]

        def run_phases(ph):
            for f in ph:
                f()

        B.barrier()
        run_phases(front(0))

        for ci in range(ALLC):
            window = ci < NW
            xh = xhome(ci)
            kh = khome(ci)
            rwhi = PA.ap()[:, 0:GC]
            rwlo = PA.ap()[:, GC:2 * GC]
            pend = front(ci + 1) if ci + 1 < ALLC else []

            def consume_to(dst):
                def consume(oi, ps, slot):
                    B.wait("scalar", B.ctr["tensor"])
                    B.a_copy(grp(dst, oi, C), ps)
                    B.psum_release(slot, "scalar")
                return consume

            B.wait("tensor", B.ctr["scalar"])
            B.wait("tensor", B.ctr["gpsimd"])
            B.wait("scalar", B.ctr["vector"])
            wstream(t_w["Wk"], G, KTH, PA.ap()[:, 0:GC], PA.ap()[:, GC:2 * GC],
                    consume_to(kh))
            if window:
                for _ in range(11):
                    if prep_steps2:
                        prep_steps2.pop(0)()

            def mix_into(mixnm):
                B.wait("vector", B.cout)
                B.wait("vector", B.ctr["scalar"])
                B.wait("vector", B.ctr["gpsimd"])
                for g in range(G):
                    B.v_stt(grp(ekbuf, g, C), dbuf[:, g * C:(g + 1) * C],
                            cols[mixnm].ap()[:, g:g + 1],
                            xtl[:, g * C1:g * C1 + C], ALU.mult, ALU.add)
                hilo_split(ekbuf, PA.ap()[:, 0:GC], PA.ap()[:, GC:2 * GC], GC)
                B.wait("tensor", B.ctr["scalar"])
                B.wait("tensor", B.ctr["gpsimd"])

            mix_into("mixv_col")
            wstream(t_w["Wv"], G, KTH, PA.ap()[:, 0:GC], PA.ap()[:, GC:2 * GC],
                    consume_to(vbuf))
            if window:
                for _ in range(11):
                    if prep_steps2:
                        prep_steps2.pop(0)()
            act_after_kv = B.ctr["scalar"].n
            if not window:
                mix_into("mixr_col")

            B.wait("vector", B.ctr["scalar"], act_after_kv)
            km = scol.ap()[:, 0:G]
            for g in range(G):
                B.v_rmax(km[:, g:g + 1], grp(kh, g, C))
            kap_new = scol.ap()[:, G:2 * G]
            resc = scol.ap()[:, 2 * G:3 * G]
            resc2 = scol.ap()[:, 3 * G:4 * G]
            B.v_tt(kap_new, kappa.ap(), km, ALU.max)
            B.v_tt(resc, kappa.ap(), kap_new, ALU.subtract)
            B.wait("scalar", B.ctr["vector"])
            B.a_act(resc2, resc, AF.Exp)
            act_kappa = B.ctr["scalar"].n
            B.wait("vector", B.ctr["scalar"], act_kappa)
            B.v_tt(ncar.ap(), ncar.ap(), resc2, ALU.mult)
            B.v_tt(dcar.ap(), dcar.ap(), resc2, ALU.mult)
            B.v_copy(kappa.ap(), kap_new)
            B.v_ts1(negkap.ap(), kap_new, -1.0, ALU.mult)
            B.wait("scalar", B.ctr["vector"])
            B.wait("scalar", B.cout)
            for g in range(G):
                B.a_act(grp(sq, g, C), grp(kh, g, C), AF.Exp,
                        bias=negkap.ap()[:, g:g + 1])
            act_ek = B.ctr["scalar"].n

            if not window:
                def consume_r(oi, ps, slot):
                    B.wait("scalar", B.ctr["tensor"])
                    B.a_copy(grp(sigr, oi, C), ps)
                    B.psum_release(slot, "scalar")

                wstream(t_w["Wr"], G, KTH, PA.ap()[:, 0:GC],
                        PA.ap()[:, GC:2 * GC], consume_r)

            ekb = sq
            bb = ekbuf
            B.wait("vector", B.ctr["scalar"], act_ek)
            B.wait("vector", B.ctr["gpsimd"])
            B.v_tt(bb[:, 0:GC], ekb[:, 0:GC], vbuf[:, 0:GC], ALU.mult)
            N3 = xtl
            D3 = dbuf
            B.v_copy(col3(N3, C1), ncar.ap().unsqueeze(2))
            B.v_copy(col3(D3, C1), dcar.ap().unsqueeze(2))
            for g in range(G):
                lam_b = lam.ap()[:, g:g + 1].broadcast_to([128, C])
                B.v_scan(N3[:, g * C1 + 1:(g + 1) * C1], lam_b,
                         grp(bb, g, C), ncar.ap()[:, g:g + 1],
                         ALU.mult, ALU.add)
                B.v_scan(D3[:, g * C1 + 1:(g + 1) * C1], lam_b,
                         grp(ekb, g, C), dcar.ap()[:, g:g + 1],
                         ALU.mult, ALU.add)
            nlast = N3[:, 0:GC1].rearrange("p (g c) -> p g c", c=C1)[:, :, C:C1]
            dlast = D3[:, 0:GC1].rearrange("p (g c) -> p g c", c=C1)[:, :, C:C1]
            B.v_copy(ncar.ap().unsqueeze(2), nlast)
            B.v_copy(dcar.ap().unsqueeze(2), dlast)

            if window and ci == NW - 1:
                cb_b = cmask.ap().broadcast_to([128, G])
                B.v_tt(ncar.ap(), ncar.ap(), cb_b, ALU.mult)
                B.v_tt(dcar.ap(), dcar.ap(), cb_b, ALU.mult)
                B.v_tt(xbcar.ap(), xbcar.ap(), cb_b, ALU.mult)
            if window:
                if ci == NW - 1:
                    while prep_steps2:
                        prep_steps2.pop(0)()
                    B.wait("sync", st_store[0])
                    B.wait("sync", st_store[1])
                else:
                    for _ in range(11):
                        if prep_steps2:
                            prep_steps2.pop(0)()
                run_phases(pend)
                continue

            dens_all = sq[:, 0:GC]
            ln_all = vbuf[:, 0:GC]
            r0_all = vbuf[:, 0:GC]
            nums_all = ekbuf[:, 0:GC]
            for g in range(G):
                dpr = D3[:, g * C1:g * C1 + C]
                B.v_stt(grp(dens_all, g, C), grp(ekb, g, C),
                        expu.ap()[:, g:g + 1], dpr, ALU.mult, ALU.add)
            for g in range(G):
                npr = N3[:, g * C1:g * C1 + C]
                B.v_stt(grp(nums_all, g, C), grp(bb, g, C),
                        expu.ap()[:, g:g + 1], npr, ALU.mult, ALU.add)
            B.wait("scalar", B.ctr["vector"])
            B.a_act(ln_all, dens_all, AF.Ln)
            B.a_act(r0_all, ln_all, AF.Exp, scale=-1.0)
            sig_all = sigr[:, 0:GC]
            B.a_act(sig_all, sigr[:, 0:GC], AF.Sigmoid)
            B.wait("vector", B.ctr["scalar"])
            xy = dens_all
            B.v_tt(xy, dens_all, r0_all, ALU.mult)
            B.v_ts2(xy, xy, -1.0, 2.0, ALU.mult, ALU.add)
            B.v_tt(xy, xy, r0_all, ALU.mult)
            B.v_tt(nums_all, nums_all, xy, ALU.mult)
            B.v_tt(nums_all, nums_all, sig_all, ALU.mult)
            B.wait("scalar", B.ctr["vector"])
            B.a_copy(rwhi, nums_all)
            B.wait("gpsimd", B.ctr["scalar"])
            B.wait("gpsimd", B.ctr["vector"])
            B.g_tt(rwlo, nums_all, rwhi, ALU.subtract)

            B.wait("vector", B.ctr["scalar"])
            B.wait("vector", B.ctr["gpsimd"])
            for g in range(G):
                B.v_stt(grp(ekbuf, g, C), grp(h_fm, g, C),
                        gacl.ap()[:, g:g + 1], grp(xh, g, C),
                        ALU.mult, ALU.add)

            def consume_wo(oi, ps, slot):
                B.wait("vector", B.ctr["tensor"])
                B.v_stt(grp(xh, oi, C), ps, gacl.ap()[:, oi:oi + 1],
                        grp(ekbuf, oi, C), ALU.mult, ALU.add)
                B.psum_release(slot, "vector")

            B.wait("tensor", B.ctr["vector"])
            B.wait("tensor", B.ctr["scalar"])
            B.wait("tensor", B.ctr["gpsimd"])
            B.wait("vector", B.ctr["gpsimd"])
            wstream(t_w["Wo"], G, KTH, rwhi, rwlo, consume_wo)

            ln_stats(xh, C, sq)
            ln_apply(xh, vbuf, C, opcb.ap(), sb_c, sq)
            h2hi = PA.ap()[:, 0:GC]
            h2lo = PA.ap()[:, GC:2 * GC]
            hilo_split(vbuf, h2hi, h2lo, GC)
            B.wait("tensor", B.ctr["scalar"])
            B.wait("tensor", B.ctr["gpsimd"])

            acc = P4.ap()
            ahi = APR.ap()[:, 0:ITB * C]
            alo = APR.ap()[:, ITB * C:2 * ITB * C]
            af32 = vbuf
            if pend:
                pend.pop(0)()
            for blk in range(NBLK):
                for it in range(ITB):
                    og = blk * ITB + it
                    ou = GI + og
                    sg_slot = "A" if it % 2 == 0 else "Cs"
                    su_slot = "B" if it % 2 == 0 else "Ds"
                    B.psum_acquire(sg_slot)
                    psg = MMSLOT[sg_slot][:, 0:C]
                    stream_group(t_w["Wgu"], og, KTH, h2hi, h2lo, psg)
                    B.psum_acquire(su_slot)
                    psu = MMSLOT[su_slot][:, 0:C]
                    stream_group(t_w["Wgu"], ou, KTH, h2hi, h2lo, psu)
                    B.wait("scalar", B.ctr["tensor"])
                    sgb = (ekbuf[:, 0:C] if it % 2 == 0
                           else ekbuf[:, 3 * C:4 * C])
                    tb = (ekbuf[:, C:2 * C] if it % 2 == 0
                          else ekbuf[:, 4 * C:5 * C])
                    B.a_act(sgb, psg, AF.Sigmoid)
                    B.wait("vector", B.ctr["scalar"])
                    B.wait("vector", B.ctr["tensor"])
                    B.v_tt(tb, sgb, psu, ALU.mult)
                    B.psum_release(su_slot, "vector")
                    B.v_tt(grp(af32, it, C), tb, psg, ALU.mult)
                    B.psum_release(sg_slot, "vector")
                    B.wait("scalar", B.ctr["vector"])
                    B.a_copy(ahi[:, it * C:(it + 1) * C], grp(af32, it, C))
                    B.wait("gpsimd", B.ctr["scalar"])
                    B.wait("gpsimd", B.ctr["vector"])
                    B.g_tt(alo[:, it * C:(it + 1) * C], grp(af32, it, C),
                           ahi[:, it * C:(it + 1) * C], ALU.subtract)
                act_split = B.ctr["scalar"].n
                gp_split = B.ctr["gpsimd"].n
                if pend:
                    pend.pop(0)()
                B.wait("tensor", B.ctr["scalar"], act_split)
                B.wait("tensor", B.ctr["gpsimd"], gp_split)
                for ht in range(G):
                    slot = SLOT4[ht % 4]
                    B.psum_acquire(slot)
                    ps = MMSLOT[slot][:, 0:C]
                    stream_group(t_wd[blk], ht, ITB, ahi, alo, ps)
                    if blk == 0:
                        B.wait("scalar", B.ctr["tensor"])
                        B.a_copy(grp(acc, ht, C), ps)
                        B.psum_release(slot, "scalar")
                    else:
                        B.wait("vector", B.ctr["tensor"])
                        B.wait("vector", B.ctr["scalar"])
                        B.v_tt(grp(acc, ht, C), ps, grp(acc, ht, C), ALU.add)
                        B.psum_release(slot, "vector")

            run_phases(pend)
            B.wait("vector", B.ctr["scalar"])
            B.wait("vector", B.ctr["gpsimd"])
            for g in range(G):
                B.v_stt(grp(sq, g, C), grp(acc, g, C),
                        gbcl.ap()[:, g:g + 1], grp(xh, g, C),
                        ALU.mult, ALU.add)
            mci = ci - NW
            B.wait("sync", B.ctr["vector"])
            dsto = d_out[:, mci * C:(mci + 1) * C].rearrange(
                "(g p) t -> p g t", p=128)
            srco = sq[:, 0:GC].rearrange("p (g c) -> p g c", c=C)
            B.dma("sync", dsto, srco, ctr=B.cout)
            n_out_dma += 1

        B.emit("sync", lambda e, s=B.cout.sem, v=16 * n_out_dma:
               e.wait_ge(s, v))

        block = ent(nc.Block())
        for engname in ENGS:
            seq = B.prog[engname]
            getattr(block, engname)(lambda e, seq=seq: [f(e) for f in seq])

    return nc



def _col(v, G):
    return np.ascontiguousarray(
        np.asarray(v, np.float32).reshape(G, 128).T).astype(np.float32)


def make_in_maps(cfg, x_full, weights, vecs):
    W, G = cfg.W, cfg.G
    in_maps = []
    for i in range(cfg.NC):
        t0 = i * cfg.TLOC
        if i == 0:
            xw = np.concatenate([x_full[0:W + 1], x_full[0:cfg.TLOC]], axis=0)
        else:
            assert t0 - W - 1 >= 0
            xw = x_full[t0 - W - 1:t0 + cfg.TLOC]
        m = {"x": np.ascontiguousarray(xw.T, dtype=np.float32)}
        for nm in ("Wk", "Wv", "Wr", "Wo", "Wgu", "Wd", "wmod"):
            m[nm] = weights[nm]
        m["c_col"] = vecs["c_col"]
        for nm in ("lnw", "lnb", "mixk", "mixv", "mixr", "td", "u"):
            m[nm + "_col"] = _col(vecs[nm], G)
        m["cmask"] = np.full((128, 1), 0.0 if i == 0 else 1.0, np.float32)
        m["eye"] = np.eye(128, dtype=np.float32)
        in_maps.append(m)
    return in_maps


def kernel(L0_attn_hidden_state, L0_attn_ln_w, L0_attn_ln_b, L0_attn_Wk,
           L0_attn_Wv, L0_attn_Wr, L0_attn_Wo, L0_attn_mix_k, L0_attn_mix_v,
           L0_attn_mix_r, L0_attn_time_decay, L0_attn_current_token_weight,
           L0_ffn_w_gate_up, L0_ffn_w_down, L0_c, L0_wmod):
    cfg = Cfg()
    x_full = np.ascontiguousarray(
        np.asarray(L0_attn_hidden_state, np.float32).reshape(cfg.T, cfg.H))
    weights = {
        "Wk": np.ascontiguousarray(np.asarray(L0_attn_Wk, np.float32)),
        "Wv": np.ascontiguousarray(np.asarray(L0_attn_Wv, np.float32)),
        "Wr": np.ascontiguousarray(np.asarray(L0_attn_Wr, np.float32)),
        "Wo": np.ascontiguousarray(np.asarray(L0_attn_Wo, np.float32)),
        "Wgu": np.ascontiguousarray(np.asarray(L0_ffn_w_gate_up, np.float32)),
        "Wd": np.ascontiguousarray(np.asarray(L0_ffn_w_down, np.float32)),
        "wmod": np.ascontiguousarray(np.asarray(L0_wmod, np.float32)),
    }
    vecs = {
        "lnw": L0_attn_ln_w, "lnb": L0_attn_ln_b, "mixk": L0_attn_mix_k,
        "mixv": L0_attn_mix_v, "mixr": L0_attn_mix_r,
        "td": L0_attn_time_decay, "u": L0_attn_current_token_weight,
        "c_col": _col(np.asarray(L0_c, np.float32).reshape(cfg.H), cfg.G),
    }
    in_maps = make_in_maps(cfg, x_full, weights, vecs)
    nc = build_nc(cfg)
    res = run_bass_kernel_spmd(nc, in_maps, core_ids=list(range(cfg.NC)))
    shards = [np.asarray(res.results[i]["out"]).T for i in range(cfg.NC)]
    out = np.concatenate(shards, axis=0).reshape(1, cfg.T, cfg.H)
    return out.astype(np.float32)
